# version 02 (final)
# baseline (speedup 1.0000x reference)
import contextlib
import numpy as np
import concourse.bass as bass
import concourse.mybir as mybir
import bass_rust as _br
from concourse.bass_utils import run_bass_kernel_spmd

F32 = mybir.dt.float32
BF16 = mybir.dt.bfloat16
AF = mybir.ActivationFunctionType
ALU = mybir.AluOpType
AX = mybir.AxisListType

SUBLAYERS = 4

B_PER_CORE = 2
T = 2048
C = 1024
H = 16
D = 64
CS = 128
NT = T // CS
GROUP = 4
NG = NT // GROUP
NCORE = 8
EPS = 1e-5
VC = float(2 ** 20 + 1)
VK = float(1.5 * 2 ** 14)


class Sched:
    def __init__(self):
        self.ops = []
        self.cnt = {}
        self.lastwait = {}

    DMA_SEMS = ("sd", "sg")

    def op(self, eng, fn, waits=(), inc=None, inc_n=1):
        ws = []
        for snm, v in tokset(waits):
            have = self.cnt.get(snm, 0)
            assert v <= have, f"wait on future: {eng} {snm} {v} > {have}"
            if snm in self.DMA_SEMS:
                v = have
            k = (eng, snm)
            if self.lastwait.get(k, 0) < v:
                self.lastwait[k] = v
                ws.append((snm, v))
        tok = None
        if inc:
            self.cnt[inc] = self.cnt.get(inc, 0) + inc_n
            tok = (inc, self.cnt[inc])
        self.ops.append((eng, fn, tuple(ws), (inc, inc_n) if inc else None))
        return tok

    def barrier(self):
        return tuple(sorted(self.cnt.items()))


def tokset(*toks):
    best = {}

    def add(t):
        if t is None:
            return
        if isinstance(t, tuple) and len(t) == 2 and isinstance(t[0], str):
            if best.get(t[0], -1) < t[1]:
                best[t[0]] = t[1]
            return
        for x in t:
            add(x)

    for t in toks:
        add(t)
    return tuple(sorted(best.items()))


later = tokset


def build_nc(sublayers):
    nc = bass.Bass("TRN2", target_bir_lowering=False)
    x_in = nc.dram_tensor("x", [B_PER_CORE, T, C], F32, kind="ExternalInput")
    wq_d = nc.dram_tensor("wq", [C, C], F32, kind="ExternalInput")
    wk_d = nc.dram_tensor("wk", [C, C], F32, kind="ExternalInput")
    wv_d = nc.dram_tensor("wv", [C, C], F32, kind="ExternalInput")
    wo_d = nc.dram_tensor("wo", [C, C], F32, kind="ExternalInput")
    alpha_d = nc.dram_tensor("alpha", [16, 1], F32, kind="ExternalInput")
    beta_d = nc.dram_tensor("beta", [16, 1], F32, kind="ExternalInput")
    wqkv_d = nc.dram_tensor("wqkv", [3 * C, C], F32, kind="ExternalInput")
    bqkv_d = nc.dram_tensor("bqkv", [3 * C], F32, kind="ExternalInput")
    wout_d = nc.dram_tensor("wout", [C, C], F32, kind="ExternalInput")
    bout_d = nc.dram_tensor("bout", [C], F32, kind="ExternalInput")
    proj_d = nc.dram_tensor("proj", [H, D, D], F32, kind="ExternalInput")
    wffn_d = nc.dram_tensor("wffn", [C, C], F32, kind="ExternalInput")
    bffn_d = nc.dram_tensor("bffn", [C], F32, kind="ExternalInput")
    c_ident = nc.dram_tensor("c_ident", [128, 128], F32, kind="ExternalInput")
    c_masks = nc.dram_tensor("c_masks", [128, 1536], F32, kind="ExternalInput")
    c_lsel = nc.dram_tensor("c_lsel", [16, 128], F32, kind="ExternalInput")
    c_rmask = nc.dram_tensor("c_rmask", [16, 8], F32, kind="ExternalInput")
    c_rep = nc.dram_tensor("c_rep", [16, 1024], F32, kind="ExternalInput")
    c_ones1 = nc.dram_tensor("c_ones1", [1, 128], F32, kind="ExternalInput")
    out_d = nc.dram_tensor("y", [B_PER_CORE, T, C], F32, kind="ExternalOutput")
    dbg_d = nc.dram_tensor("dbg", [128, 32], F32, kind="ExternalOutput")
    dbg2_d = nc.dram_tensor("dbg2", [128, 4096], F32, kind="ExternalOutput")
    park_d = nc.dram_tensor("park", [B_PER_CORE, T, C], F32)
    wqkvT_h = nc.dram_tensor("wqkvT_h", [C, 3 * C], BF16)
    woutT_h = nc.dram_tensor("woutT_h", [C, C], BF16)
    wq8T_h = nc.dram_tensor("wq8T_h", [C, C], BF16)
    cc_in = [nc.dram_tensor(f"cc_in{i}", [1, 8], F32) for i in range(2)]
    cc_out = [nc.dram_tensor(f"cc_out{i}", [1, 8], F32) for i in range(2)]

    with contextlib.ExitStack() as stack:
        def sb(name, shape, dt=BF16):
            return stack.enter_context(nc.sbuf_tensor(name, shape, dt))

        x_act = sb("x_act", [128, NT * C], F32)
        arena = sb("arena", [128, 45056], BF16)
        f32a = sb("f32a", [128, 5120], F32)
        masks = sb("masksb", [128, 1536], BF16)
        ident = sb("identb", [128, 128], BF16)
        S_f32 = sb("S_f32", [128, 512], F32)
        S_bf = sb("S_bf", [128, 512], BF16)
        kv_f32 = sb("kv_f32", [128, 1040], F32)
        kv_bf = sb("kv_bf", [128, 1040], BF16)
        projh = sb("projh", [128, 512], BF16)
        a2col = sb("a2col", [128, 8], F32)
        bb_bcast = sb("bb_bcast", [128, 1024], BF16)
        vbias_b = sb("vbias_b", [128, 1024], BF16)
        fbias_b = sb("fbias_b", [128, 1024], BF16)
        obias_b = sb("obias_b", [128, 1024], BF16)
        bqk_cols = sb("bqk_cols", [128, 16], F32)
        stats_m = sb("stats_m", [128, 32], F32)
        stats_r = sb("stats_r", [128, 32], F32)
        lncols = sb("lncols", [128, 16], F32)
        amax_c = sb("amax_c", [128, 4], F32)
        qcols = sb("qcols", [128, 2], F32)
        cell = sb("cell", [1, 16], F32)
        arow = sb("arow", [1, 128], F32)
        ones1p = sb("ones1p", [1, 128], F32)
        xln_bf = sb("xln_bf", [128, 1024], BF16)
        a16 = f32a[0:16, 2312:2313]
        b16 = f32a[0:16, 2313:2314]
        lsel = f32a[0:16, 1024:1152]
        rmask = f32a[0:16, 1152:1160]
        repm = f32a[0:16, 0:1024]
        ones1 = f32a[0:1, 1160:1288]
        rowt = f32a[0:1, 1288:2312]

        pA = stack.enter_context(nc.psum_tensor("pA", [128, 1024], F32))
        pB = stack.enter_context(nc.psum_tensor("pB", [128, 1024], F32))
        p4 = stack.enter_context(nc.psum_tensor("p4", [128, 512], F32))
        pT = stack.enter_context(nc.psum_tensor("pTt", [128, 1024], BF16))
        pj = stack.enter_context(nc.psum_tensor("pj", [128, 1024], F32))

        sems = {}
        for snm in ("sd", "sg", "st", "sv", "ss", "scc"):
            sems[snm] = stack.enter_context(nc.semaphore(snm))

        S = Sched()

        def Top(fn, waits=()):
            return S.op("tensor", fn, waits, "st")

        def Vop(fn, waits=()):
            return S.op("vector", fn, waits, "sv")

        def Sop(fn, waits=()):
            return S.op("scalar", fn, waits, "ss")

        def Dop(fn, waits=()):
            return S.op("sync", fn, waits, "sd", 16)

        def Gdma(fn, waits=()):
            return S.op("gpsimd", fn, waits, "sg", 16)

        def Gcop(fn, waits=()):
            return S.op("gpsimd", fn, waits, "sg", 1)

        t_ident = Gdma(lambda g: g.dma_start(out=ident[:, :], in_=c_ident.ap()))
        t_masks = Gdma(lambda g: g.dma_start(out=masks[:, :], in_=c_masks.ap()))
        t_lsel = Dop(lambda d: d.dma_start(out=lsel[:, :], in_=c_lsel.ap()))
        t_rmask = Dop(lambda d: d.dma_start(out=rmask[:, :], in_=c_rmask.ap()))
        t_rep = Dop(lambda d: d.dma_start(out=repm[:, :], in_=c_rep.ap()))
        t_ones = Dop(lambda d: d.dma_start(out=ones1[:, :], in_=c_ones1.ap()))
        t_onesp = Dop(lambda d: d.dma_start(out=ones1p[:, :], in_=c_ones1.ap()))
        t_al = Dop(lambda d: d.dma_start(out=a16[:, :], in_=alpha_d.ap()))
        t_be = Dop(lambda d: d.dma_start(out=b16[:, :], in_=beta_d.ap()))
        t_bqk = Dop(lambda d: d.dma_start(out=bqk_cols[:, :],
                                          in_=bqkv_d.ap()[0:2048].rearrange("(j p) -> p j", p=128),
                                          allow_slow_non_contiguous=True))
        t_projh = Gdma(lambda g: g.dma_start(
            out=projh[:, :].rearrange("p (j m) -> p j m", m=64),
            in_=proj_d.ap().rearrange("(j two) d m -> (two d) j m", two=2),
            allow_slow_non_contiguous=True))

        def sigmoid_half(dst16, src16, wait):
            zc = cell[0:16, 8:9] if False else None
            z = f32a[0:16, 2314:2315]
            p = f32a[0:16, 2315:2316]
            t0 = Vop(lambda v: v.tensor_scalar(z[:, :], src16, -1.0 / 1024.0, None, op0=ALU.mult), waits=[wait])
            tk = Vop(lambda v: v.tensor_scalar(p[:, :], z[:, :], 1.0 / 6.0, 1.0, op0=ALU.mult, op1=ALU.add),
                     waits=[t0])
            for kk in (5, 4, 3, 2, 1):
                tk = Vop(lambda v, kk=kk: v.scalar_tensor_tensor(p[:, :], z[:, :], 1.0 / kk, p[:, :],
                                                                 op0=ALU.mult, op1=ALU.mult), waits=[tk])
                tk = Vop(lambda v: v.tensor_scalar(p[:, :], p[:, :], 1.0, None, op0=ALU.add), waits=[tk])
            for _ in range(10):
                tk = Vop(lambda v: v.tensor_tensor(p[:, :], p[:, :], p[:, :], op=ALU.mult), waits=[tk])
            tk = Vop(lambda v: v.tensor_scalar(p[:, :], p[:, :], 1.0, None, op0=ALU.add), waits=[tk])
            tk = Vop(lambda v: v.reciprocal(p[:, :], p[:, :]), waits=[tk])
            return Vop(lambda v, d=dst16: v.tensor_scalar(d, p[:, :], 0.5, None, op0=ALU.mult), waits=[tk])

        t_a2 = sigmoid_half(a16[0:16, :], a16[0:16, :], t_al)
        t_b2 = sigmoid_half(b16[0:16, :], b16[0:16, :], t_be)
        t_ar = Vop(lambda v: v.tensor_scalar(rmask[0:16, :], rmask[0:16, :], a16[0:16, :], None, op0=ALU.mult),
                   waits=[t_a2, t_rmask])
        t_mm1 = Top(lambda t: t.matmul(pA[:, 0:8], lsel[0:16, :], rmask[0:16, :], start=True, stop=True),
                    waits=[t_ar, t_lsel])
        t_a2c = Vop(lambda v: v.tensor_copy(a2col[:, :], pA[:, 0:8]), waits=[t_mm1])
        t_br1 = Top(lambda t: t.matmul(pA[0:1, 512:1024], b16[0:16, :], repm[0:16, 0:512], start=True, stop=True),
                    waits=[t_b2, t_rep, t_a2c])
        t_br2 = Top(lambda t: t.matmul(pB[0:1, 0:512], b16[0:16, :], repm[0:16, 512:1024], start=True, stop=True))
        t_brow1 = Vop(lambda v: v.tensor_copy(rowt[0:1, 0:512], pA[0:1, 512:1024]), waits=[t_br1])
        t_brow2 = Vop(lambda v: v.tensor_copy(rowt[0:1, 512:1024], pB[0:1, 0:512]), waits=[t_br2])
        t_bb1 = Top(lambda t: t.matmul(pA[:, 0:512], ones1[0:1, :], rowt[0:1, 0:512], start=True, stop=True),
                    waits=[t_brow1, t_brow2, t_ones])
        t_bb2 = Top(lambda t: t.matmul(pA[:, 512:1024], ones1[0:1, :], rowt[0:1, 512:1024], start=True, stop=True))
        t_bb = Vop(lambda v: v.tensor_copy(bb_bcast[:, :], pA[:, :]), waits=[t_bb1, t_bb2])

        def build_bcast(dst, row_ap, after):
            tl = Dop(lambda d, r=row_ap: d.dma_start(out=rowt[0:1, :], in_=r), waits=[after])
            m1 = Top(lambda t: t.matmul(pA[:, 0:512], ones1[0:1, :], rowt[0:1, 0:512], start=True, stop=True),
                     waits=[tl, after])
            m2 = Top(lambda t: t.matmul(pA[:, 512:1024], ones1[0:1, :], rowt[0:1, 512:1024], start=True, stop=True))
            return Vop(lambda v, a=dst: v.tensor_copy(a, pA[:, :]), waits=[m1, m2])

        t_vb = build_bcast(vbias_b[:, :], bqkv_d.ap()[2048:3072].rearrange("(one c) -> one c", one=1), t_bb)
        t_fb = build_bcast(fbias_b[:, :], bffn_d.ap().rearrange("(one c) -> one c", one=1), t_vb)
        t_ob = build_bcast(obias_b[:, :], bout_d.ap().rearrange("(one c) -> one c", one=1), t_fb)

        wstage = arena[:, 0:8192]
        wf32 = f32a[:, 0:4096]
        u_ap = f32a[:, 4096:5120]
        t_am0 = Vop(lambda v: v.memset(amax_c[:, 1:2], 0.0))
        t_prev = t_ob
        for half in range(2):
            tl = Gdma(lambda g, hh=half: g.dma_start(
                out=wf32[:, :].rearrange("p (a c) -> p a c", c=1024),
                in_=wffn_d.ap().rearrange("(a p) c -> p a c", p=128)[:, hh * 4:(hh + 1) * 4, :]),
                waits=[t_prev])
            tr = Vop(lambda v: v.tensor_reduce(lncols[:, 0:1], wf32[:, :], AX.X, ALU.max,
                                               apply_absolute_value=True), waits=[tl, t_am0])
            t_prev = Vop(lambda v: v.tensor_tensor(amax_c[:, 1:2], amax_c[:, 1:2], lncols[:, 0:1], op=ALU.max),
                         waits=[tr])
        t_par0 = Gdma(lambda g: g.dma_start(out=arow[0:1, :], in_=amax_c[:, 1:2],
                                            allow_slow_non_contiguous=True), waits=[t_prev])
        t_par = Vop(lambda v: v.tensor_reduce(cell[0:1, 0:1], arow[0:1, :], AX.X, ALU.max), waits=[t_par0])
        t_sw1 = Vop(lambda v: v.tensor_scalar(cell[0:1, 1:2], cell[0:1, 0:1], 1e-12, None, op0=ALU.max),
                    waits=[t_par])
        t_sw2 = Vop(lambda v: v.reciprocal(cell[0:1, 2:3], cell[0:1, 1:2]), waits=[t_sw1])
        t_sw = Vop(lambda v: v.tensor_scalar(cell[0:1, 3:4], cell[0:1, 2:3], 448.0, None, op0=ALU.mult),
                   waits=[t_sw2])
        t_swb_m = Top(lambda t: t.matmul(pB[:, 512:513], ones1p[0:1, :], cell[0:1, 3:4], start=True, stop=True),
                      waits=[t_sw, t_brow2, t_onesp])
        t_swb = Vop(lambda v: v.tensor_copy(amax_c[:, 2:3], pB[:, 512:513]), waits=[t_swb_m])
        tq_done = t_swb
        for half in range(2):
            tl = Gdma(lambda g, hh=half: g.dma_start(
                out=wf32[:, :].rearrange("p (a c) -> p a c", c=1024),
                in_=wffn_d.ap().rearrange("(a p) c -> p a c", p=128)[:, hh * 4:(hh + 1) * 4, :]),
                waits=[tq_done])
            ts1 = Vop(lambda v: v.tensor_scalar(wf32[:, :], wf32[:, :], amax_c[:, 2:3], None, op0=ALU.mult),
                      waits=[tl])
            prev = ts1
            for strip in range(4):
                slc = slice(strip * 1024, (strip + 1) * 1024)
                t_u = Vop(lambda v, slc=slc: v.tensor_scalar(u_ap[:, 0:1024], wf32[:, slc], VK, VK,
                                                             op0=ALU.add, op1=ALU.subtract), waits=[prev])
                t_p = Vop(lambda v, slc=slc: v.tensor_scalar(wf32[:, slc], u_ap[:, 0:1024], VC, None,
                                                             op0=ALU.mult), waits=[t_u])
                t_d = Vop(lambda v, slc=slc: v.tensor_tensor(u_ap[:, 0:1024], u_ap[:, 0:1024], wf32[:, slc],
                                                             op=ALU.subtract), waits=[t_p])
                prev = Vop(lambda v, slc=slc, hh=half: v.tensor_tensor(
                    wstage[:, hh * 4096 + slc.start: hh * 4096 + slc.start + 1024],
                    u_ap[:, 0:1024], wf32[:, slc], op=ALU.add), waits=[t_d])
            tq_done = prev
        S.mark_wq = len(S.ops)
        pT_free = None
        tw_last = tq_done
        for a_i in range(8):
            trs = []
            for cb in range(8):
                trs.append(Top(lambda t, a_i=a_i, cb=cb: t.transpose(
                    pT[:, cb * 128: cb * 128 + 128],
                    wstage[:, a_i * 1024 + cb * 128: a_i * 1024 + cb * 128 + 128],
                    ident[:, :]), waits=[tq_done, t_ident, pT_free] if cb == 0 else ()))
            ev = Vop(lambda v: v.tensor_copy(xln_bf[:, 0:1024], pT[:, :]), waits=[trs[-1], tw_last])
            pT_free = ev
            tw_last = Dop(lambda d, a_i=a_i: d.dma_start(
                out=wq8T_h.ap().rearrange("(cb j) o -> j cb o", j=128)[:, :, a_i * 128:(a_i + 1) * 128],
                in_=xln_bf[:, 0:1024].rearrange("j (cb o) -> j cb o", o=128)), waits=[ev])
        t_wq8 = tw_last

        def build_wT(dst_h, src_d, rows, after):
            tok = after
            pTf = pT_free
            for a_i in range(rows // 128):
                tl = Gdma(lambda g, a_i=a_i, sdd=src_d: g.dma_start(
                    out=xln_bf[:, 0:1024], in_=sdd.ap()[a_i * 128:(a_i + 1) * 128, :]), waits=[tok])
                trs = []
                for cb in range(8):
                    trs.append(Top(lambda t, cb=cb: t.transpose(
                        pT[:, cb * 128: cb * 128 + 128],
                        xln_bf[:, cb * 128: cb * 128 + 128], ident[:, :]),
                        waits=[tl, pTf] if cb == 0 else ()))
                ev = Vop(lambda v: v.tensor_copy(wstage[:, 0:1024], pT[:, :]), waits=[trs[-1], tok])
                pTf = ev
                tok = Dop(lambda d, a_i=a_i, dd=dst_h: d.dma_start(
                    out=dd.ap().rearrange("(cb j) r -> j cb r", j=128)[:, :, a_i * 128:(a_i + 1) * 128],
                    in_=wstage[:, 0:1024].rearrange("j (cb o) -> j cb o", o=128)), waits=[ev])
            return tok, pTf

        t_wqkvT, pT_free = build_wT(wqkvT_h, wqkv_d, 3 * C, t_wq8)
        S.mark_prep = None
        t_woutT, pT_free = build_wT(woutT_h, wout_d, C, t_wqkvT)
        t_prep = t_woutT
        S.mark_prep = len(S.ops)

        x_tok = [None] * NT

        def load_seq(s, from_park, after):
            src = park_d if from_park else x_in
            toks = []
            for ti in range(NT):
                toks.append(Dop(lambda d, s=s, ti=ti, src=src: d.dma_start(
                    out=x_act[:, ti * C:(ti + 1) * C],
                    in_=src.ap()[s, ti * 128:(ti + 1) * 128, :]),
                    waits=[after, x_tok[ti]]))
                x_tok[ti] = toks[-1]
            return toks

        def park_seq(s):
            toks = []
            for ti in range(NT):
                toks.append(Dop(lambda d, s=s, ti=ti: d.dma_start(
                    out=park_d.ap()[s, ti * 128:(ti + 1) * 128, :],
                    in_=x_act[:, ti * C:(ti + 1) * C]), waits=[x_tok[ti]]))
            return toks

        def out_seq(s):
            toks = []
            for ti in range(NT):
                toks.append(Dop(lambda d, s=s, ti=ti: d.dma_start(
                    out=out_d.ap()[s, ti * 128:(ti + 1) * 128, :],
                    in_=x_act[:, ti * C:(ti + 1) * C]), waits=[x_tok[ti]]))
            return toks

        trash = f32a[:, 4096:5120]

        def newton_sqrt(dst, w_ap, t1_ap, h_ap, wait, iters=5):
            t0 = Vop(lambda v, h=h_ap, w=w_ap: v.tensor_scalar(h, w, 0.5, None, op0=ALU.mult), waits=[wait])
            tk = Vop(lambda v, d=dst, w=w_ap: v.tensor_scalar(d, w, 1.0, 0.5, op0=ALU.add, op1=ALU.mult),
                     waits=[t0])
            for _ in range(iters):
                tk = Vop(lambda v, t1=t1_ap, d=dst: v.reciprocal(t1, d), waits=[tk])
                tk = Vop(lambda v, t1=t1_ap, h=h_ap: v.tensor_tensor(t1, h, t1, op=ALU.mult), waits=[tk])
                tk = Vop(lambda v, d=dst, t1=t1_ap: v.scalar_tensor_tensor(d, d, 0.5, t1,
                                                                           op0=ALU.mult, op1=ALU.add),
                         waits=[tk])
            return tk

        def phase_stats(base, extra_wait=None):
            mcols = stats_m[:, base:base + NT]
            rcols = stats_r[:, base:base + NT]
            toks = [None]
            for ti in range(NT):
                xt = x_act[:, ti * C:(ti + 1) * C]
                w0 = [x_tok[ti], extra_wait, toks[-1]]
                toks.append(Vop(lambda v, xt=xt, ti=ti, m=mcols: v.tensor_reduce(
                    m[:, ti:ti + 1], xt, AX.X, ALU.add), waits=w0))
                toks.append(Vop(lambda v, xt=xt, ti=ti: v.scalar_tensor_tensor(
                    trash[:, 0:1024], xt, 1.0, xt, op0=ALU.mult, op1=ALU.mult,
                    accum_out=lncols[:, ti:ti + 1]), waits=[toks[-1]]))
            tm = Vop(lambda v, m=mcols: v.tensor_scalar(m, m, 1.0 / C, None, op0=ALU.mult), waits=[toks[-1]])
            t4 = Vop(lambda v, m=mcols: v.tensor_tensor(trash[:, 0:16], m, m, op=ALU.mult), waits=[tm])
            t5 = Vop(lambda v: v.scalar_tensor_tensor(trash[:, 16:32], lncols[:, 0:16], 1.0 / C,
                                                      trash[:, 0:16], op0=ALU.mult, op1=ALU.subtract),
                     waits=[t4])
            t6 = Vop(lambda v: v.tensor_scalar(trash[:, 16:32], trash[:, 16:32], EPS, None, op0=ALU.add),
                     waits=[t5])
            t7 = Vop(lambda v: v.reciprocal(trash[:, 16:32], trash[:, 16:32]), waits=[t6])
            tr = newton_sqrt(rcols, trash[:, 16:32], trash[:, 32:48], trash[:, 48:64], t7)
            return tokset(tm, tr)

        def attn_a_seq(first, after_tok, pT_free0):
            A = 0

            def take(n):
                nonlocal A
                ap = arena[:, A:A + n]
                A += n
                assert A <= 45056
                return ap

            wslot = [take(8192), take(8192)]
            xlnT = take(4096)
            qT = take(4096)
            kTn = take(4096)
            kTnA = take(4096)
            ktokn = take(4096)
            vb = take(4096)
            Yt = xlnT[:, 0:1024]
            YT = xlnT[:, 1024:2048]
            Xb = [wslot[0][:, i * 1024:(i + 1) * 1024] for i in range(4)]
            ub = [wslot[0][:, 4096:4608], wslot[0][:, 4608:5120]]
            PTb = wslot[0][:, 5120:6144]
            ktokr = f32a[:, 0:4096]

            tz1 = Vop(lambda v: v.memset(S_f32[:, :], 0.0), waits=[after_tok])
            tz2 = Sop(lambda s: s.copy(S_bf[:, :], S_f32[:, :]), waits=[tz1])
            sbf_tok = tz2

            stats_tok = phase_stats(0, after_tok)
            xlnbf_free = [None]
            wnames = [wq_d, wk_d, wv_d, wo_d]
            wfree = {0: after_tok, 1: after_tok}
            bank = {"pA": after_tok, "pB": after_tok, "p4": after_tok,
                    "pT": tokset(pT_free0, after_tok), "pj0": after_tok, "pj1": after_tok}
            ktokn_tok = [None] * GROUP
            tile_done = [None] * NT

            for g in range(NG):
                taus = list(range(g * GROUP, (g + 1) * GROUP))
                tq = Gdma(lambda gg, wd=wnames[0]: gg.dma_start(
                    out=wslot[0][:, :].rearrange("p (ci c) -> p ci c", c=1024),
                    in_=wd.ap().rearrange("(ci p) c -> p ci c", p=128)), waits=[wfree[0]])
                tk = Gdma(lambda gg, wd=wnames[1]: gg.dma_start(
                    out=wslot[1][:, :].rearrange("p (ci c) -> p ci c", c=1024),
                    in_=wd.ap().rearrange("(ci p) c -> p ci c", p=128)), waits=[wfree[1]])

                xlnT_toks = []
                for il, ti in enumerate(taus):
                    t_ln = Vop(lambda v, ti=ti: v.tensor_scalar(
                        xln_bf[:, :], x_act[:, ti * C:(ti + 1) * C],
                        stats_m[:, ti:ti + 1], stats_r[:, ti:ti + 1],
                        op0=ALU.subtract, op1=ALU.mult),
                        waits=[stats_tok, x_tok[ti], xlnbf_free[0]])
                    trs = []
                    for cb in range(8):
                        trs.append(Top(lambda t, cb=cb: t.transpose(
                            pT[:, cb * 128: cb * 128 + 128],
                            xln_bf[:, cb * 128: cb * 128 + 128], ident[:, :]),
                            waits=[t_ln, bank["pT"], t_ident] if cb == 0 else ()))
                    xlnbf_free[0] = trs[-1]
                    ev = Vop(lambda v, il=il: v.tensor_copy(
                        xlnT[:, :].rearrange("p (ci tt) -> p ci tt", tt=512)[:, :, il * 128:(il + 1) * 128],
                        pT[:, :].rearrange("p (ci tt) -> p ci tt", tt=128)), waits=[trs[-1]])
                    bank["pT"] = ev
                    xlnT_toks.append(ev)
                xlnT_all = later(*xlnT_toks)

                q_toks = []
                for di in range(8):
                    pjx = pj[:, 0:512] if di % 2 == 0 else pj[:, 512:1024]
                    bf = "pj0" if di % 2 == 0 else "pj1"
                    mms = []
                    for ci in range(8):
                        mms.append(Top(lambda t, di=di, ci=ci, pjx=pjx: t.matmul(
                            pjx, wslot[0][:, ci * 1024 + di * 128: ci * 1024 + di * 128 + 128],
                            xlnT[:, ci * 512:(ci + 1) * 512], start=(ci == 0), stop=(ci == 7)),
                            waits=[tq, xlnT_all, bank[bf]] if ci == 0 else ()))
                    ev = Sop(lambda s, di=di, pjx=pjx: s.copy(qT[:, di * 512:(di + 1) * 512], pjx),
                             waits=[mms[-1]])
                    bank[bf] = ev
                    q_toks.append(ev)
                q_done = later(*q_toks)
                q_mm_done = q_toks[-1]
                tv = Gdma(lambda gg, wd=wnames[2]: gg.dma_start(
                    out=wslot[0][:, :].rearrange("p (ci c) -> p ci c", c=1024),
                    in_=wd.ap().rearrange("(ci p) c -> p ci c", p=128)), waits=[q_mm_done])

                kgrp_toks = []
                for il in range(GROUP):
                    evs = []
                    for oh in range(2):
                        pjx = pj[:, 0:512] if oh == 0 else pj[:, 512:1024]
                        bf = "pj0" if oh == 0 else "pj1"
                        mms = []
                        for ci in range(8):
                            mms.append(Top(lambda t, il=il, oh=oh, ci=ci, pjx=pjx: t.matmul(
                                pjx, xlnT[:, ci * 512 + il * 128: ci * 512 + il * 128 + 128],
                                wslot[1][:, ci * 1024 + oh * 512: ci * 1024 + oh * 512 + 512],
                                start=(ci == 0), stop=(ci == 7)),
                                waits=[tk, xlnT_all, bank[bf]] if ci == 0 else ()))
                        ev = Vop(lambda v, il=il, oh=oh, pjx=pjx: v.tensor_copy(
                            ktokr[:, il * 1024 + oh * 512: il * 1024 + oh * 512 + 512], pjx), waits=[mms[-1]])
                        bank[bf] = ev
                        evs.append(ev)
                    nrm2 = lncols[:, 0:16]
                    rk = trash[:, 64:80]
                    t1 = Vop(lambda v, il=il: v.tensor_tensor(
                        trash[:, 0:1024], ktokr[:, il * 1024:(il + 1) * 1024],
                        ktokr[:, il * 1024:(il + 1) * 1024], op=ALU.mult),
                        waits=[evs[0], evs[1]])
                    t2 = Vop(lambda v: v.tensor_reduce(
                        nrm2, trash[:, 0:1024].rearrange("p (h dd) -> p h dd", dd=64), AX.X, ALU.add),
                        waits=[t1])
                    t3 = Vop(lambda v: v.tensor_scalar(nrm2, nrm2, 1e-24, None, op0=ALU.max),
                             waits=[t2])
                    t4 = Vop(lambda v: v.reciprocal(nrm2, nrm2), waits=[t3])
                    t5 = newton_sqrt(rk, nrm2, trash[:, 80:96], trash[:, 96:112], t4)
                    nts = []
                    for h in range(H):
                        nts.append(Vop(lambda v, il=il, h=h, rk=rk: v.tensor_scalar(
                            ktokn[:, il * 1024 + h * 64: il * 1024 + h * 64 + 64],
                            ktokr[:, il * 1024 + h * 64: il * 1024 + h * 64 + 64],
                            rk[:, h:h + 1], None, op0=ALU.mult), waits=[t5]))
                    kn = later(*nts)
                    ktokn_tok[il] = kn
                    trs = []
                    for j in range(8):
                        trs.append(Top(lambda t, il=il, j=j: t.transpose(
                            pT[:, j * 128: j * 128 + 128],
                            ktokn[:, il * 1024 + j * 128: il * 1024 + j * 128 + 128], ident[:, :]),
                            waits=[kn, bank["pT"]] if j == 0 else ()))
                    e1 = Vop(lambda v, il=il: v.tensor_copy(
                        kTn[:, :].rearrange("p (j tt) -> p j tt", tt=512)[:, :, il * 128: il * 128 + 128],
                        pT[:, :].rearrange("p (j tt) -> p j tt", tt=128)), waits=[trs[-1]])
                    e2s = []
                    for j in range(8):
                        e2s.append(Vop(lambda v, il=il, j=j: v.tensor_scalar(
                            kTnA[:, j * 512 + il * 128: j * 512 + il * 128 + 128],
                            pT[:, j * 128:(j + 1) * 128],
                            a2col[:, j:j + 1], None, op0=ALU.mult),
                            waits=[trs[-1], t_a2c] if j == 0 else ()))
                    bank["pT"] = tokset(e1, e2s[-1])
                    kgrp_toks.append(tokset(e1, e2s[-1]))
                k_done = later(*kgrp_toks)
                k_mm_done = kgrp_toks[-1]
                to = Gdma(lambda gg, wd=wnames[3]: gg.dma_start(
                    out=wslot[1][:, :].rearrange("p (ci c) -> p ci c", c=1024),
                    in_=wd.ap().rearrange("(ci p) c -> p ci c", p=128)), waits=[k_mm_done])

                v_toks = []
                for il in range(GROUP):
                    for oh in range(2):
                        pjx = pj[:, 0:512] if oh == 0 else pj[:, 512:1024]
                        bf = "pj0" if oh == 0 else "pj1"
                        mms = []
                        for ci in range(8):
                            mms.append(Top(lambda t, il=il, oh=oh, ci=ci, pjx=pjx: t.matmul(
                                pjx, xlnT[:, ci * 512 + il * 128: ci * 512 + il * 128 + 128],
                                wslot[0][:, ci * 1024 + oh * 512: ci * 1024 + oh * 512 + 512],
                                start=(ci == 0), stop=(ci == 7)),
                                waits=[tv, bank[bf]] if ci == 0 else ()))
                        ev = Vop(lambda v, il=il, oh=oh, pjx=pjx: v.tensor_tensor(
                            vb[:, il * 1024 + oh * 512: il * 1024 + oh * 512 + 512], pjx,
                            bb_bcast[:, oh * 512:(oh + 1) * 512], op=ALU.mult), waits=[mms[-1], t_bb])
                        bank[bf] = ev
                        v_toks.append(ev)
                v_done = later(*v_toks)

                for il, ti in enumerate(taus):
                    for b in range(2):
                        heads = list(range(b * 8, b * 8 + 8))
                        perm = [0, 2, 4, 6, 1, 3, 5, 7]
                        inv = [0, 4, 1, 5, 2, 6, 3, 7]
                        gmm = []
                        for sl in range(8):
                            i = perm[sl]
                            h = heads[i]
                            ph = (h % 2) * 64
                            jj = h // 2
                            gmm.append(Top(lambda t, sl=sl, ph=ph, jj=jj, il=il: t.matmul(
                                pA[:, sl * 128:(sl + 1) * 128],
                                kTnA[ph:ph + 64, jj * 512 + il * 128: jj * 512 + il * 128 + 128],
                                kTn[ph:ph + 64, jj * 512 + il * 128: jj * 512 + il * 128 + 128],
                                start=True, stop=True),
                                waits=[k_done, bank["pA"], t_masks] if sl == 0 else ()))
                        mN = []
                        for hl in range(2):
                            mN.append(Vop(lambda v, hl=hl: v.tensor_tensor(
                                Xb[0][:, hl * 512:(hl + 1) * 512], pA[:, hl * 512:(hl + 1) * 512],
                                masks[:, 0:512], op=ALU.mult), waits=[gmm[-1]]))
                            mN.append(Vop(lambda v, hl=hl: v.tensor_tensor(
                                Xb[1][:, hl * 512:(hl + 1) * 512], pA[:, hl * 512:(hl + 1) * 512],
                                masks[:, 512:1024], op=ALU.mult), waits=[gmm[-1]]))
                        bank["pA"] = later(*mN)
                        cur_tok = bank["pA"]
                        curX, curXT, nxtX, nxtXT = Xb[0], Xb[1], Xb[2], Xb[3]
                        ks = []
                        for sl in range(8):
                            i = perm[sl]
                            h = heads[i]
                            ph = (h % 2) * 64
                            jj = h // 2
                            dst = p4[:, sl * 64:(sl + 1) * 64] if sl < 4 else pj[:, (sl - 4) * 64:(sl - 3) * 64]
                            w0 = [sbf_tok, bank["p4"], bank["pj0"]] if sl == 0 else ()
                            ks.append(Top(lambda t, dst=dst, ph=ph, jj=jj, il=il: t.matmul(
                                dst,
                                kTnA[ph:ph + 64, jj * 512 + il * 128: jj * 512 + il * 128 + 128],
                                S_bf[ph:ph + 64, jj * 64:(jj + 1) * 64], start=True, stop=True),
                                waits=w0))
                        vview = vb[:, il * 1024 + b * 512: il * 1024 + b * 512 + 512].rearrange(
                            "p (four two d) -> p four two d", two=2, d=64)
                        tR1 = Vop(lambda v, vview=vview: v.tensor_tensor(
                            ub[0][:, 0:256].rearrange("p (four d) -> p four d", d=64),
                            vview[:, :, 0, :], p4[:, 0:256].rearrange("p (four d) -> p four d", d=64),
                            op=ALU.subtract), waits=[ks[-1], v_done])
                        tR2 = Vop(lambda v, vview=vview: v.tensor_tensor(
                            ub[0][:, 256:512].rearrange("p (four d) -> p four d", d=64),
                            vview[:, :, 1, :], pj[:, 0:256].rearrange("p (four d) -> p four d", d=64),
                            op=ALU.subtract), waits=[ks[-1], v_done])
                        tR = later(tR1, tR2)
                        bank["p4"] = tR
                        bank["pj0"] = tR
                        ucur, unxt = ub[0], ub[1]
                        u_tok = tR
                        for st_i in range(7):
                            ap_mms = []
                            for i in range(8):
                                ap_mms.append(Top(lambda t, i=i, cxt=curXT, uc=ucur: t.matmul(
                                    p4[:, i * 64:(i + 1) * 64],
                                    cxt[:, i * 128:(i + 1) * 128],
                                    uc[:, i * 64:(i + 1) * 64], start=True, stop=True),
                                    waits=[cur_tok, u_tok, bank["p4"]] if i == 0 else ()))
                            pw_ev = None
                            if st_i < 5:
                                pw = []
                                for i in range(8):
                                    pw.append(Top(lambda t, i=i, cx=curX, cxt=curXT: t.matmul(
                                        pB[:, i * 128:(i + 1) * 128],
                                        cxt[:, i * 128:(i + 1) * 128],
                                        cx[:, i * 128:(i + 1) * 128], start=True, stop=True),
                                        waits=[cur_tok, bank["pB"]] if i == 0 else ()))
                                for i in range(8):
                                    pw.append(Top(lambda t, i=i, cx=curX, cxt=curXT: t.matmul(
                                        pA[:, i * 128:(i + 1) * 128],
                                        cx[:, i * 128:(i + 1) * 128],
                                        cxt[:, i * 128:(i + 1) * 128], start=True, stop=True),
                                        waits=[bank["pA"]] if i == 0 else ()))
                                e1 = Vop(lambda v, nx=nxtX: v.tensor_copy(nx[:, :], pB[:, :]), waits=[pw[7]])
                                e2 = Sop(lambda s, nxt=nxtXT: s.copy(nxt[:, :], pA[:, :]), waits=[pw[-1]])
                                bank["pB"] = e1
                                bank["pA"] = e2
                                pw_ev = tokset(e1, e2)
                            elif st_i == 5:
                                pw = []
                                for i in range(8):
                                    pw.append(Top(lambda t, i=i, cx=curX, cxt=curXT: t.matmul(
                                        pB[:, i * 128:(i + 1) * 128],
                                        cx[:, i * 128:(i + 1) * 128],
                                        cxt[:, i * 128:(i + 1) * 128], start=True, stop=True),
                                        waits=[cur_tok, bank["pB"]] if i == 0 else ()))
                                e2 = Sop(lambda s, nxt=nxtXT: s.copy(nxt[:, :], pB[:, :]), waits=[pw[-1]])
                                bank["pB"] = e2
                                pw_ev = tokset(e2)
                            opx = ALU.subtract if st_i == 0 else ALU.add
                            tu = Vop(lambda v, opx=opx, un=unxt, uc=ucur: v.tensor_tensor(
                                un[:, :], uc[:, :], p4[:, :], op=opx), waits=[ap_mms[-1]])
                            bank["p4"] = tu
                            u_tok = tu
                            ucur, unxt = unxt, ucur
                            if st_i < 6 and pw_ev is not None:
                                curX, nxtX = nxtX, curX
                                curXT, nxtXT = nxtXT, curXT
                                cur_tok = pw_ev
                        W_sb = ucur
                        W_tok = u_tok
                        a2 = []
                        for sl in range(8):
                            i = perm[sl]
                            h = heads[i]
                            ph = (h % 2) * 64
                            jj = h // 2
                            a2.append(Top(lambda t, sl=sl, ph=ph, jj=jj, il=il: t.matmul(
                                pA[:, sl * 128:(sl + 1) * 128],
                                kTn[ph:ph + 64, jj * 512 + il * 128: jj * 512 + il * 128 + 128],
                                qT[ph:ph + 64, jj * 512 + il * 128: jj * 512 + il * 128 + 128],
                                start=True, stop=True),
                                waits=[q_done, bank["pA"]] if sl == 0 else ()))
                        tp1 = Vop(lambda v: v.tensor_tensor(PTb[:, 0:512], pA[:, 0:512],
                                                            masks[:, 512:1024], op=ALU.mult), waits=[a2[-1]])
                        tp2 = Vop(lambda v: v.tensor_tensor(PTb[:, 512:1024], pA[:, 512:1024],
                                                            masks[:, 512:1024], op=ALU.mult), waits=[a2[-1]])
                        bank["pA"] = later(tp1, tp2)
                        pt_tok = bank["pA"]
                        ym = []
                        for sl in range(8):
                            i = perm[sl]
                            h = heads[i]
                            ph = (h % 2) * 64
                            jj = h // 2
                            off = sl * 64 if sl < 4 else 512 + (sl - 4) * 64
                            ym.append(Top(lambda t, off=off, ph=ph, jj=jj, il=il: t.matmul(
                                pB[:, off:off + 64],
                                qT[ph:ph + 64, jj * 512 + il * 128: jj * 512 + il * 128 + 128],
                                S_bf[ph:ph + 64, jj * 64:(jj + 1) * 64], start=True, stop=True),
                                waits=[sbf_tok, bank["pB"]] if sl == 0 else ()))
                        ymi = []
                        for i in range(8):
                            ymi.append(Top(lambda t, i=i, W=W_sb: t.matmul(
                                p4[:, i * 64:(i + 1) * 64],
                                PTb[:, inv[i] * 128:(inv[i] + 1) * 128],
                                W[:, inv[i] * 64:(inv[i] + 1) * 64], start=True, stop=True),
                                waits=[pt_tok, W_tok, bank["p4"]] if i == 0 else ()))
                        yt_ev = Yt[:, b * 512: b * 512 + 512].rearrange("p (four two d) -> p four two d",
                                                                       two=2, d=64)
                        th1 = Sop(lambda s, yt_ev=yt_ev: s.copy(
                            yt_ev[:, :, 0, :], pB[:, 0:256].rearrange("p (four d) -> p four d", d=64)),
                            waits=[ym[-1]])
                        th2 = Sop(lambda s, yt_ev=yt_ev: s.copy(
                            yt_ev[:, :, 1, :], pB[:, 512:768].rearrange("p (four d) -> p four d", d=64)),
                            waits=[ym[-1]])
                        bank["pB"] = later(th1, th2)
                        ty = Vop(lambda v, b=b: v.tensor_tensor(
                            Yt[:, b * 512:(b + 1) * 512], Yt[:, b * 512:(b + 1) * 512],
                            p4[:, 0:512], op=ALU.add), waits=[th1, th2, ymi[-1]])
                        bank["p4"] = ty
                        su = []
                        for pass_par in (0, 1):
                            for i in range(pass_par, 8, 2):
                                h = heads[i]
                                jl = i // 2
                                if pass_par == 0:
                                    dst = p4[0:64, jl * 64:(jl + 1) * 64]
                                    tp = None
                                else:
                                    dst = pj[64:128, jl * 64:(jl + 1) * 64]
                                    tp = (0, 64)
                                w0 = [W_tok, ktokn_tok[il], bank["p4"], bank["pj0"]] if (pass_par == 0 and i == 0) else ()
                                su.append(Top(lambda t, dst=dst, h=h, i=i, tp=tp, il=il, W=W_sb: t.matmul(
                                    dst,
                                    ktokn[:, il * 1024 + h * 64: il * 1024 + h * 64 + 64],
                                    W[:, inv[i] * 64:(inv[i] + 1) * 64], start=True, stop=True,
                                    tile_position=tp), waits=w0))
                        tsu1 = Vop(lambda v, b=b: v.tensor_tensor(
                            S_f32[0:64, b * 256:(b + 1) * 256], S_f32[0:64, b * 256:(b + 1) * 256],
                            p4[0:64, 0:256], op=ALU.add), waits=[su[-1]])
                        tsu2 = Vop(lambda v, b=b: v.tensor_tensor(
                            S_f32[64:128, b * 256:(b + 1) * 256], S_f32[64:128, b * 256:(b + 1) * 256],
                            pj[64:128, 0:256], op=ALU.add), waits=[su[-1]])
                        bank["p4"] = tsu1
                        bank["pj0"] = tsu2
                        tsc = Sop(lambda s, b=b: s.copy(S_bf[:, b * 256:(b + 1) * 256],
                                                        S_f32[:, b * 256:(b + 1) * 256]),
                                  waits=[tsu1, tsu2, ym[-1]])
                        sbf_tok = tsc

                    ytr = []
                    for j in range(8):
                        ytr.append(Top(lambda t, j=j: t.transpose(
                            pT[:, j * 128:(j + 1) * 128],
                            Yt[:, j * 128:(j + 1) * 128], ident[:, :]),
                            waits=[bank["pB"], bank["pT"]] if j == 0 else ()))
                    evy = Vop(lambda v: v.tensor_copy(YT[:, :], pT[:, :]), waits=[ytr[-1]])
                    bank["pT"] = evy
                    rtoks = []
                    for oh in range(2):
                        pjx = pj[:, 0:512] if oh == 0 else pj[:, 512:1024]
                        bf = "pj0" if oh == 0 else "pj1"
                        mms = []
                        for ci in range(8):
                            mms.append(Top(lambda t, oh=oh, ci=ci, pjx=pjx: t.matmul(
                                pjx, YT[:, ci * 128:(ci + 1) * 128],
                                wslot[1][:, ci * 1024 + oh * 512: ci * 1024 + oh * 512 + 512],
                                start=(ci == 0), stop=(ci == 7)),
                                waits=[to, evy, bank[bf]] if ci == 0 else ()))
                        ev = Vop(lambda v, ti=ti, oh=oh, pjx=pjx: v.scalar_tensor_tensor(
                            x_act[:, ti * C + oh * 512: ti * C + oh * 512 + 512],
                            pjx, 1.0, x_act[:, ti * C + oh * 512: ti * C + oh * 512 + 512],
                            op0=ALU.mult, op1=ALU.add), waits=[mms[-1]])
                        bank[bf] = ev
                        rtoks.append(ev)
                        wo_mm_done = mms[-1]
                    x_tok[ti] = later(*rtoks)
                    tile_done[ti] = x_tok[ti]
                wfree[0] = tokset(sbf_tok, bank["p4"], bank["pB"], x_tok[taus[-1]])
                wfree[1] = wo_mm_done
            return tile_done, bank["pT"]

        def ffn_pass1(base, after):
            stats_tok = phase_stats(base, after)
            toks = [stats_tok]
            for ti in range(NT):
                t_mx = Vop(lambda v, ti=ti, base=base: v.tensor_scalar(
                    trash[:, 0:1024], x_act[:, ti * C:(ti + 1) * C],
                    stats_m[:, base + ti: base + ti + 1], -1e30,
                    op0=ALU.subtract, op1=ALU.max,
                    accum_out=amax_c[:, 2:3]), waits=[toks[-1], x_tok[ti]])
                t_mn = Vop(lambda v, ti=ti, base=base: v.tensor_scalar(
                    trash[:, 0:1024], x_act[:, ti * C:(ti + 1) * C],
                    stats_m[:, base + ti: base + ti + 1], 1e30,
                    op0=ALU.subtract, op1=ALU.min,
                    accum_out=amax_c[:, 3:4]), waits=[t_mx])
                t_ng = Vop(lambda v: v.tensor_scalar(amax_c[:, 3:4], amax_c[:, 3:4], -1.0, None,
                                                     op0=ALU.mult), waits=[t_mn])
                toks.append(Vop(lambda v, ti=ti: v.tensor_tensor(
                    lncols[:, ti:ti + 1], amax_c[:, 2:3], amax_c[:, 3:4], op=ALU.max), waits=[t_ng]))
            t1 = Vop(lambda v, base=base: v.tensor_tensor(
                lncols[:, 0:16], lncols[:, 0:16], stats_r[:, base:base + 16], op=ALU.mult),
                waits=[toks[-1]])
            t2 = Vop(lambda v: v.tensor_reduce(trash[:, 16:17], lncols[:, 0:16], AX.X, ALU.max),
                     waits=[t1])
            t3 = Vop(lambda v: v.tensor_tensor(amax_c[:, 0:1], amax_c[:, 0:1], trash[:, 16:17],
                                               op=ALU.max), waits=[t2])
            return t3

        def ffn_amax_finish(idx, after):
            t0 = Gdma(lambda g: g.dma_start(out=arow[0:1, :], in_=amax_c[:, 0:1],
                                            allow_slow_non_contiguous=True), waits=[after])
            t1 = Vop(lambda v: v.tensor_reduce(cell[0:1, 4:5], arow[0:1, :], AX.X, ALU.max),
                     waits=[t0])
            t1b = Vop(lambda v: v.memset(cell[0:1, 5:12], 0.0), waits=[t1])
            t2 = Dop(lambda d, idx=idx: d.dma_start(out=cc_in[idx].ap(), in_=cell[0:1, 4:12]),
                     waits=[t1, t1b])
            t3 = S.op("gpsimd", lambda g, idx=idx: g.collective_compute(
                "AllReduce", ALU.max, replica_groups=[list(range(NCORE))],
                ins=[cc_in[idx].ap().opt()], outs=[cc_out[idx].ap().opt()]),
                waits=[t2], inc="scc")
            t4 = Gdma(lambda g, idx=idx: g.dma_start(out=cell[0:1, 5:6], in_=cc_out[idx].ap()[0:1, 0:1]),
                      waits=[t3])
            t5 = Vop(lambda v: v.tensor_scalar(cell[0:1, 6:7], cell[0:1, 5:6], 1e-12, None, op0=ALU.max),
                     waits=[t4])
            t6 = Vop(lambda v: v.reciprocal(cell[0:1, 6:7], cell[0:1, 6:7]), waits=[t5])
            t7 = Vop(lambda v: v.tensor_scalar(cell[0:1, 6:7], cell[0:1, 6:7], 448.0, None, op0=ALU.mult),
                     waits=[t6])
            t8 = Vop(lambda v: v.tensor_tensor(cell[0:1, 7:8], cell[0:1, 6:7], cell[0:1, 3:4],
                                               op=ALU.mult), waits=[t7])
            t9 = Vop(lambda v: v.reciprocal(cell[0:1, 8:9], cell[0:1, 7:8]), waits=[t8])
            m1 = Top(lambda t: t.matmul(pB[:, 0:1], ones1p[0:1, :], cell[0:1, 6:7], start=True, stop=True),
                     waits=[t7, t_onesp, S.barrier()])
            m2 = Top(lambda t: t.matmul(pB[:, 512:513], ones1p[0:1, :], cell[0:1, 8:9],
                                        start=True, stop=True), waits=[t9])
            b1 = Vop(lambda v: v.tensor_copy(qcols[:, 0:1], pB[:, 0:1]), waits=[m1])
            b2 = Vop(lambda v: v.tensor_copy(qcols[:, 1:2], pB[:, 512:513]), waits=[m2])
            return tokset(b1, b2)

        def ffn_pass2(base, after, out_seq):
            A = 0

            def take(n):
                nonlocal A
                ap = arena[:, A:A + n]
                A += n
                assert A <= 45056
                return ap

            wq8 = take(8192)
            qx = take(1024)
            qxT = take(1024)
            twq = Dop(lambda d: d.dma_start(
                out=wq8[:, :].rearrange("p (ci o) -> p ci o", o=1024),
                in_=wq8T_h.ap().rearrange("(ci p) o -> p ci o", p=128)), waits=[after])
            trs = Vop(lambda v, base=base: v.tensor_scalar(
                lncols[:, 0:16], stats_r[:, base:base + 16], qcols[:, 0:1], None, op0=ALU.mult),
                waits=[after])
            bank = {"pT": S.barrier(), "pj0": S.barrier(), "pj1": S.barrier()}
            tq_free = after
            tqT_free = after
            out_toks = []
            for ti in range(NT):
                tf = f32a[:, 0:1024]
                uf = f32a[:, 1024:2048]
                pf = f32a[:, 2048:3072]
                t1 = Vop(lambda v, ti=ti, base=base: v.tensor_scalar(
                    tf[:, :], x_act[:, ti * C:(ti + 1) * C],
                    stats_m[:, base + ti: base + ti + 1], lncols[:, ti:ti + 1],
                    op0=ALU.subtract, op1=ALU.mult), waits=[trs, x_tok[ti]])
                t2 = Vop(lambda v: v.tensor_scalar(uf[:, :], tf[:, :], VK, VK,
                                                   op0=ALU.add, op1=ALU.subtract), waits=[t1])
                t3 = Vop(lambda v: v.tensor_scalar(pf[:, :], uf[:, :], VC, None, op0=ALU.mult),
                         waits=[t2])
                t4 = Vop(lambda v: v.tensor_tensor(uf[:, :], uf[:, :], pf[:, :], op=ALU.subtract),
                         waits=[t3])
                t5 = Vop(lambda v: v.tensor_tensor(qx[:, :], uf[:, :], pf[:, :], op=ALU.add),
                         waits=[t4, tq_free])
                qtrs = []
                for cb in range(8):
                    qtrs.append(Top(lambda t, cb=cb: t.transpose(
                        pT[:, cb * 128: cb * 128 + 128],
                        qx[:, cb * 128: cb * 128 + 128], ident[:, :]),
                        waits=[t5, bank["pT"], twq] if cb == 0 else ()))
                tq_free = qtrs[-1]
                evq = Vop(lambda v: v.tensor_copy(qxT[:, :], pT[:, :]), waits=[qtrs[-1], tqT_free])
                bank["pT"] = evq
                rtoks = []
                for oh in range(2):
                    pjx = pj[:, 0:512] if oh == 0 else pj[:, 512:1024]
                    bf = "pj0" if oh == 0 else "pj1"
                    mms = []
                    for ci in range(8):
                        mms.append(Top(lambda t, oh=oh, ci=ci, pjx=pjx: t.matmul(
                            pjx, qxT[:, ci * 128:(ci + 1) * 128],
                            wq8[:, ci * 1024 + oh * 512: ci * 1024 + oh * 512 + 512],
                            start=(ci == 0), stop=(ci == 7)),
                            waits=[evq, bank[bf]] if ci == 0 else ()))
                    tb = Vop(lambda v, ti=ti, oh=oh: v.tensor_tensor(
                        trash[:, oh * 512:(oh + 1) * 512], fbias_b[:, oh * 512:(oh + 1) * 512],
                        x_act[:, ti * C + oh * 512: ti * C + oh * 512 + 512], op=ALU.add),
                        waits=[mms[0]])
                    ev = Vop(lambda v, ti=ti, oh=oh, pjx=pjx: v.scalar_tensor_tensor(
                        x_act[:, ti * C + oh * 512: ti * C + oh * 512 + 512],
                        pjx, qcols[:, 1:2], trash[:, oh * 512:(oh + 1) * 512],
                        op0=ALU.mult, op1=ALU.add), waits=[mms[-1], tb])
                    bank[bf] = ev
                    rtoks.append(ev)
                tqT_free = rtoks[-1]
                x_tok[ti] = later(*rtoks)
                if out_seq is not None:
                    out_toks.append(Dop(lambda d, ti=ti, out_seq=out_seq: d.dma_start(
                        out=out_d.ap()[out_seq, ti * 128:(ti + 1) * 128, :],
                        in_=x_act[:, ti * C:(ti + 1) * C]), waits=[x_tok[ti]]))
            return out_toks

        def attn_b_seq(after_tok, pT_free0, dump_z=False):
            A = 0

            def take(n):
                nonlocal A
                ap = arena[:, A:A + n]
                A += n
                assert A <= 45056
                return ap

            wslot = [take(8192), take(8192)]
            xlnT = take(4096)
            qf = take(4096)
            kf = take(4096)
            vaug = take(4160)
            z = take(1024)
            zT = take(1024)
            qpT = take(1024)
            kpT = take(1024)
            kptok = take(1024)
            aw2m = take(1024)
            C_sb = f32a[:, 0:1024]

            stats_tok = phase_stats(0, after_tok)
            tz1 = Vop(lambda v: v.memset(kv_f32[:, :], 0.0), waits=[after_tok])
            tz2 = Sop(lambda s: s.copy(kv_bf[:, :], kv_f32[:, :]), waits=[tz1])
            kv_tok = tz2
            tz3s = []
            for blk in range(64):
                tz3s.append(Vop(lambda v, blk=blk: v.memset(
                    vaug[:, blk * 65 + 64: blk * 65 + 65], 1.0), waits=[after_tok] if blk == 0 else ()))
            tz3 = tz3s[-1]

            wsrc = [(wqkvT_h, 0), (wqkvT_h, 1024), (wqkvT_h, 2048), (woutT_h, 0)]
            wfree = {0: after_tok, 1: after_tok}
            bank = {"pA": after_tok, "pB": after_tok, "p4": after_tok,
                    "pT": tokset(pT_free0, after_tok), "pj0": after_tok, "pj1": after_tok}
            xlnbf_free = [None]
            tile_done = [None] * NT

            def load_w(slotidx, widx, extra=None):
                dsrc, coff = wsrc[widx]
                return Dop(lambda d, dsrc=dsrc, coff=coff, slotidx=slotidx: d.dma_start(
                    out=wslot[slotidx][:, :].rearrange("p (ci c) -> p ci c", c=1024),
                    in_=dsrc.ap().rearrange("(ci p) o -> p ci o", p=128)[:, :, coff:coff + 1024]),
                    waits=[wfree[slotidx], extra])

            for g in range(NG):
                taus = list(range(g * GROUP, (g + 1) * GROUP))
                tq = load_w(0, 0)
                tk = load_w(1, 1)
                xlnT_toks = []
                for il, ti in enumerate(taus):
                    t_ln = Vop(lambda v, ti=ti: v.tensor_scalar(
                        xln_bf[:, :], x_act[:, ti * C:(ti + 1) * C],
                        stats_m[:, ti:ti + 1], stats_r[:, ti:ti + 1],
                        op0=ALU.subtract, op1=ALU.mult),
                        waits=[stats_tok, x_tok[ti], xlnbf_free[0]])
                    ttrs = []
                    for cb in range(8):
                        ttrs.append(Top(lambda t, cb=cb: t.transpose(
                            pT[:, cb * 128: cb * 128 + 128],
                            xln_bf[:, cb * 128: cb * 128 + 128], ident[:, :]),
                            waits=[t_ln, bank["pT"]] if cb == 0 else ()))
                    xlnbf_free[0] = ttrs[-1]
                    ev = Vop(lambda v, il=il: v.tensor_copy(
                        xlnT[:, :].rearrange("p (ci tt) -> p ci tt", tt=512)[:, :, il * 128:(il + 1) * 128],
                        pT[:, :].rearrange("p (ci tt) -> p ci tt", tt=128)), waits=[ttrs[-1]])
                    bank["pT"] = ev
                    xlnT_toks.append(ev)
                xlnT_all = later(*xlnT_toks)

                def fproj(dstbuf, slotidx, wtok, bcol_base, extra=None):
                    toks = []
                    for di in range(8):
                        pjx = pj[:, 0:512] if di % 2 == 0 else pj[:, 512:1024]
                        bf = "pj0" if di % 2 == 0 else "pj1"
                        mms = []
                        for ci in range(8):
                            mms.append(Top(lambda t, di=di, ci=ci, pjx=pjx, slotidx=slotidx: t.matmul(
                                pjx, wslot[slotidx][:, ci * 1024 + di * 128: ci * 1024 + di * 128 + 128],
                                xlnT[:, ci * 512:(ci + 1) * 512], start=(ci == 0), stop=(ci == 7)),
                                waits=[wtok, xlnT_all, bank[bf], extra] if ci == 0 else ()))
                        ev = Vop(lambda v, di=di, pjx=pjx, dstbuf=dstbuf, bcol_base=bcol_base: v.tensor_scalar(
                            dstbuf[:, di * 512:(di + 1) * 512], pjx,
                            bqk_cols[:, bcol_base + di: bcol_base + di + 1], None, op0=ALU.add),
                            waits=[mms[-1], t_bqk])
                        bank[bf] = ev
                        toks.append(ev)
                    return toks

                q_toks = fproj(qf, 0, tq, 0)
                tv = load_w(0, 2, q_toks[-1])
                k_toks = fproj(kf, 1, tk, 8)
                to = load_w(1, 3, k_toks[-1])
                q_done = later(*q_toks)
                k_done = later(*k_toks)

                v_toks = []
                for il in range(GROUP):
                    for oh in range(2):
                        pjx = pj[:, 0:512] if oh == 0 else pj[:, 512:1024]
                        bf = "pj0" if oh == 0 else "pj1"
                        mms = []
                        for ci in range(8):
                            mms.append(Top(lambda t, il=il, oh=oh, ci=ci, pjx=pjx: t.matmul(
                                pjx, xlnT[:, ci * 512 + il * 128: ci * 512 + il * 128 + 128],
                                wslot[0][:, ci * 1024 + oh * 512: ci * 1024 + oh * 512 + 512],
                                start=(ci == 0), stop=(ci == 7)),
                                waits=[tv, bank[bf]] if ci == 0 else ()))
                        ev = Vop(lambda v, il=il, oh=oh, pjx=pjx: v.tensor_tensor(
                            vaug[:, il * 1040: il * 1040 + 1040].rearrange(
                                "p (h e) -> p h e", e=65)[:, oh * 8:(oh + 1) * 8, 0:64],
                            pjx.rearrange("p (h dd) -> p h dd", dd=64),
                            vbias_b[:, oh * 512:(oh + 1) * 512].rearrange("p (h dd) -> p h dd", dd=64),
                            op=ALU.add), waits=[mms[-1], t_vb, tz3])
                        bank[bf] = ev
                        v_toks.append(ev)
                v_done = later(*v_toks)
                wfree[0] = v_toks[-1]

                for il, ti in enumerate(taus):
                    for b in range(2):
                        heads = list(range(b * 8, b * 8 + 8))
                        perm = [0, 2, 4, 6, 1, 3, 5, 7]
                        qmm, kmm = [], []
                        for sl in range(8):
                            i = perm[sl]
                            h = heads[i]
                            ph = (h % 2) * 64
                            jj = h // 2
                            qmm.append(Top(lambda t, sl=sl, ph=ph, jj=jj, il=il: t.matmul(
                                pA[0:64, sl * 128:(sl + 1) * 128],
                                projh[ph:ph + 64, jj * 64:(jj + 1) * 64],
                                qf[ph:ph + 64, jj * 512 + il * 128: jj * 512 + il * 128 + 128],
                                start=True, stop=True),
                                waits=[q_done, t_projh, bank["pA"]] if sl == 0 else ()))
                        for sl in range(8):
                            i = perm[sl]
                            h = heads[i]
                            ph = (h % 2) * 64
                            jj = h // 2
                            kmm.append(Top(lambda t, sl=sl, ph=ph, jj=jj, il=il: t.matmul(
                                pB[0:64, sl * 128:(sl + 1) * 128],
                                projh[ph:ph + 64, jj * 64:(jj + 1) * 64],
                                kf[ph:ph + 64, jj * 512 + il * 128: jj * 512 + il * 128 + 128],
                                start=True, stop=True),
                                waits=[k_done, bank["pB"]] if sl == 0 else ()))
                        eq = Vop(lambda v: v.tensor_scalar(qpT[0:64, 0:1024], pA[0:64, 0:1024], 0.0, None,
                                                           op0=ALU.max), waits=[qmm[-1]])
                        ek = Vop(lambda v: v.tensor_scalar(kpT[0:64, 0:1024], pB[0:64, 0:1024], 0.0, None,
                                                           op0=ALU.max), waits=[kmm[-1]])
                        bank["pA"] = eq
                        bank["pB"] = ek
                        ktr = []
                        for sl in range(8):
                            ktr.append(Top(lambda t, sl=sl: t.transpose(
                                pT[:, sl * 64: sl * 64 + 64],
                                kpT[0:64, sl * 128:(sl + 1) * 128], ident[0:64, 0:64]),
                                waits=[ek, bank["pT"]] if sl == 0 else ()))
                        ekt = Vop(lambda v: v.tensor_copy(kptok[:, 0:512], pT[:, 0:512]),
                                  waits=[ktr[-1]])
                        bank["pT"] = ekt
                        amm = []
                        for sl in range(8):
                            amm.append(Top(lambda t, sl=sl: t.matmul(
                                pA[:, sl * 128:(sl + 1) * 128],
                                kpT[0:64, sl * 128:(sl + 1) * 128],
                                qpT[0:64, sl * 128:(sl + 1) * 128],
                                start=True, stop=True),
                                waits=[eq, ek, bank["pA"]] if sl == 0 else ()))
                        am1 = Vop(lambda v: v.tensor_tensor(aw2m[:, 0:512], pA[:, 0:512],
                                                            masks[:, 1024:1536], op=ALU.mult),
                                  waits=[amm[-1]])
                        am2 = Vop(lambda v: v.tensor_tensor(aw2m[:, 512:1024], pA[:, 512:1024],
                                                            masks[:, 1024:1536], op=ALU.mult),
                                  waits=[amm[-1]])
                        bank["pA"] = later(am1, am2)
                        hmm_ = []
                        for sl in range(8):
                            kvp = b * 8 + sl
                            off = sl * 66 if sl < 4 else 512 + (sl - 4) * 66
                            hmm_.append(Top(lambda t, sl=sl, kvp=kvp, off=off: t.matmul(
                                pB[:, off:off + 65],
                                qpT[0:64, sl * 128:(sl + 1) * 128],
                                kv_bf[0:64, kvp * 65:(kvp + 1) * 65],
                                start=True, stop=True),
                                waits=[kv_tok, eq, bank["pB"]] if sl == 0 else ()))
                        imm = []
                        for sl in range(8):
                            h = heads[perm[sl]]
                            dst = (p4[:, sl * 66: sl * 66 + 65] if sl < 4
                                   else pj[:, (sl - 4) * 66:(sl - 4) * 66 + 65])
                            w0 = [later(am1, am2), v_done, bank["p4"], bank["pj0"]] if sl == 0 else ()
                            imm.append(Top(lambda t, sl=sl, h=h, dst=dst, il=il: t.matmul(
                                dst, aw2m[:, sl * 128:(sl + 1) * 128],
                                vaug[:, il * 1040 + h * 65: il * 1040 + (h + 1) * 65],
                                start=True, stop=True), waits=w0))
                        cview = C_sb[:, 0:520].rearrange("p (s e) -> p s e", e=65)
                        c1 = Vop(lambda v, cview=cview: v.tensor_copy(
                            cview[:, 0:4, :], pB[:, 0:264].rearrange("p (s e) -> p s e", e=66)[:, :, 0:65]),
                            waits=[hmm_[-1]])
                        c2 = Vop(lambda v, cview=cview: v.tensor_copy(
                            cview[:, 4:8, :],
                            pB[:, 512:776].rearrange("p (s e) -> p s e", e=66)[:, :, 0:65]),
                            waits=[hmm_[-1]])
                        bank["pB"] = later(c1, c2)
                        c3 = Vop(lambda v, cview=cview: v.tensor_tensor(
                            cview[:, 0:4, :], cview[:, 0:4, :],
                            p4[:, 0:264].rearrange("p (s e) -> p s e", e=66)[:, :, 0:65],
                            op=ALU.add), waits=[c1, imm[3]])
                        c4 = Vop(lambda v, cview=cview: v.tensor_tensor(
                            cview[:, 4:8, :], cview[:, 4:8, :],
                            pj[:, 0:264].rearrange("p (s e) -> p s e", e=66)[:, :, 0:65],
                            op=ALU.add), waits=[c2, imm[-1]])
                        bank["p4"] = c3
                        bank["pj0"] = c4
                        den = f32a[:, 1040:1048]
                        rec = f32a[:, 1040:1048]
                        d0 = Vop(lambda v: v.memset(f32a[:, 1040:1056], 1.0), waits=[c3, c4])
                        d1s = [d0]
                        for sl in range(8):
                            d1s.append(Vop(lambda v, sl=sl, den=den: v.tensor_scalar(
                                den[:, sl:sl + 1], C_sb[:, sl * 65 + 64: sl * 65 + 65],
                                1e-6, None, op0=ALU.add), waits=[d0] if sl == 0 else ()))
                        d2 = Vop(lambda v: v.reciprocal(f32a[:, 1040 + 16:1040 + 32] if False else f32a[:, 1040:1056][:, 0:16], f32a[:, 1040:1056]), waits=[d1s[-1]])
                        zts = []
                        for sl in range(8):
                            h = heads[perm[sl]]
                            zts.append(Vop(lambda v, sl=sl, h=h, rec=rec: v.tensor_scalar(
                                z[:, h * 64:(h + 1) * 64], C_sb[:, sl * 65: sl * 65 + 64],
                                rec[:, sl:sl + 1], None, op0=ALU.mult), waits=[d2]))
                        z_tok = later(*zts)
                        nmm = []
                        for sl in range(8):
                            h = heads[perm[sl]]
                            dst = (p4[0:64, sl * 66: sl * 66 + 65] if sl < 4
                                   else pj[0:64, (sl - 4) * 66:(sl - 4) * 66 + 65])
                            w0 = [ekt, v_done, bank["p4"], bank["pj0"]] if sl == 0 else ()
                            nmm.append(Top(lambda t, sl=sl, h=h, dst=dst, il=il: t.matmul(
                                dst, kptok[:, sl * 64:(sl + 1) * 64],
                                vaug[:, il * 1040 + h * 65: il * 1040 + (h + 1) * 65],
                                start=True, stop=True), waits=w0))
                        base_c = b * 520
                        kv1 = Vop(lambda v, base_c=base_c: v.tensor_tensor(
                            kv_f32[0:64, base_c: base_c + 260].rearrange("p (s e) -> p s e", e=65),
                            kv_f32[0:64, base_c: base_c + 260].rearrange("p (s e) -> p s e", e=65),
                            p4[0:64, 0:264].rearrange("p (s e) -> p s e", e=66)[:, :, 0:65],
                            op=ALU.add), waits=[nmm[-1]])
                        kv2 = Vop(lambda v, base_c=base_c: v.tensor_tensor(
                            kv_f32[0:64, base_c + 260: base_c + 520].rearrange("p (s e) -> p s e", e=65),
                            kv_f32[0:64, base_c + 260: base_c + 520].rearrange("p (s e) -> p s e", e=65),
                            pj[0:64, 0:264].rearrange("p (s e) -> p s e", e=66)[:, :, 0:65],
                            op=ALU.add), waits=[nmm[-1]])
                        bank["p4"] = kv1
                        bank["pj0"] = kv2
                        kv_tok = Sop(lambda s, b=b: s.copy(kv_bf[0:64, b * 520:(b + 1) * 520],
                                                           kv_f32[0:64, b * 520:(b + 1) * 520]),
                                     waits=[kv1, kv2, hmm_[-1]])
                    if dump_z and ti == 1:
                        Gdma(lambda d: d.dma_start(out=dbg2_d.ap()[:, 0:1024], in_=z[:, :]),
                             waits=[z_tok])
                    ztr = []
                    for cb in range(8):
                        ztr.append(Top(lambda t, cb=cb: t.transpose(
                            pT[:, cb * 128: cb * 128 + 128],
                            z[:, cb * 128: cb * 128 + 128], ident[:, :]),
                            waits=[z_tok, bank["pT"]] if cb == 0 else ()))
                    evz = Vop(lambda v: v.tensor_copy(zT[:, :], pT[:, :]), waits=[ztr[-1]])
                    bank["pT"] = evz
                    rtoks = []
                    for oh in range(2):
                        pjx = pj[:, 0:512] if oh == 0 else pj[:, 512:1024]
                        bf = "pj0" if oh == 0 else "pj1"
                        mms = []
                        for ci in range(8):
                            mms.append(Top(lambda t, oh=oh, ci=ci, pjx=pjx: t.matmul(
                                pjx, zT[:, ci * 128:(ci + 1) * 128],
                                wslot[1][:, ci * 1024 + oh * 512: ci * 1024 + oh * 512 + 512],
                                start=(ci == 0), stop=(ci == 7)),
                                waits=[to, evz, bank[bf]] if ci == 0 else ()))
                        tb = Vop(lambda v, ti=ti, oh=oh: v.tensor_tensor(
                            trash[:, oh * 512:(oh + 1) * 512], obias_b[:, oh * 512:(oh + 1) * 512],
                            x_act[:, ti * C + oh * 512: ti * C + oh * 512 + 512], op=ALU.add),
                            waits=[mms[0], t_ob])
                        ev = Vop(lambda v, ti=ti, oh=oh, pjx=pjx: v.scalar_tensor_tensor(
                            x_act[:, ti * C + oh * 512: ti * C + oh * 512 + 512],
                            pjx, 1.0, trash[:, oh * 512:(oh + 1) * 512],
                            op0=ALU.mult, op1=ALU.add), waits=[mms[-1], tb])
                        bank[bf] = ev
                        rtoks.append(ev)
                    x_tok[ti] = later(*rtoks)
                    tile_done[ti] = x_tok[ti]
                wfree[1] = mms[-1]
            return tile_done, bank["pT"]

        load_seq(0, False, t_prep)
        done0, pT_free = attn_a_seq(True, S.barrier(), pT_free)
        park0 = park_seq(0)
        load_seq(1, False, S.barrier())
        done1, pT_free = attn_a_seq(False, S.barrier(), pT_free)

        if sublayers == 1:
            o1 = out_seq(1)
            o0 = []
            for ti in range(NT):
                o0.append(Dop(lambda d, ti=ti: d.dma_start(
                    out=out_d.ap()[0, ti * 128:(ti + 1) * 128, :],
                    in_=park_d.ap()[0, ti * 128:(ti + 1) * 128, :]), waits=[park0[ti]]))

        if sublayers >= 2:
            za = Vop(lambda v: v.memset(amax_c[:, 0:1], 0.0), waits=[S.barrier()])
            ffn_pass1(16, tokset(za, S.barrier()))
            park_seq(1)
            load_seq(0, True, S.barrier())
            ffn_pass1(0, S.barrier())
            ffn_amax_finish(0, S.barrier())
            out0 = 0 if sublayers == 2 else None
            ffn_pass2(0, S.barrier(), out0)
            park_seq(0)
            load_seq(1, True, S.barrier())
            ffn_pass2(16, S.barrier(), 1 if sublayers == 2 else None)

        if sublayers >= 3:
            done1b, pT_free = attn_b_seq(S.barrier(), pT_free)
            park1b = park_seq(1)
            load_seq(0, True, S.barrier())
            done0b, pT_free = attn_b_seq(S.barrier(), pT_free, dump_z=(sublayers == 3))
            if sublayers == 3:
                out_seq(0)
                for ti in range(NT):
                    Dop(lambda d, ti=ti: d.dma_start(
                        out=out_d.ap()[1, ti * 128:(ti + 1) * 128, :],
                        in_=park_d.ap()[1, ti * 128:(ti + 1) * 128, :]), waits=[park1b[ti]])

        if sublayers >= 4:
            za = Vop(lambda v: v.memset(amax_c[:, 0:1], 0.0), waits=[S.barrier()])
            ffn_pass1(0, tokset(za, S.barrier()))
            park_seq(0)
            load_seq(1, True, S.barrier())
            ffn_pass1(16, S.barrier())
            ffn_amax_finish(1, S.barrier())
            ffn_pass2(16, S.barrier(), 1)
            load_seq(0, True, S.barrier())
            ffn_pass2(0, S.barrier(), 0)

        bfin = S.barrier()
        if sublayers == 3:
            Gdma(lambda d: d.dma_start(out=dbg2_d.ap()[:, 1040:1056], in_=f32a[:, 1040:1056]), waits=[bfin])
            Gdma(lambda d: d.dma_start(out=dbg2_d.ap()[:, 1056:2080], in_=f32a[:, 0:1024]), waits=[bfin])
            Gdma(lambda d: d.dma_start(out=dbg2_d.ap()[:, 2080:3104], in_=arena[:, 34880:35904]), waits=[bfin])
            Gdma(lambda d: d.dma_start(out=dbg2_d.ap()[:, 3104:4096], in_=arena[:, 32832:33824]), waits=[bfin])
            Gdma(lambda d: d.dma_start(out=dbg2_d.ap()[:, 2080:3104], in_=arena[:, 33856:34880]), waits=[bfin])
        if sublayers == 2:
            Gdma(lambda d: d.dma_start(out=dbg2_d.ap()[:, 0:1024], in_=arena[:, 8192:9216]), waits=[bfin])
            Gdma(lambda d: d.dma_start(out=dbg2_d.ap()[:, 1024:2048], in_=arena[:, 9216:10240]), waits=[bfin])
            Gdma(lambda d: d.dma_start(out=dbg2_d.ap()[:, 2048:3072], in_=arena[:, 0:1024]), waits=[bfin])
            Dop(lambda d: d.dma_start(out=dbg2_d.ap()[:, 3072:4096], in_=f32a[:, 0:1024]), waits=[bfin])
        Dop(lambda d: d.dma_start(out=dbg_d.ap()[0:1, 0:16], in_=cell[0:1, :]), waits=[bfin])
        Dop(lambda d: d.dma_start(out=dbg_d.ap()[:, 16:18], in_=qcols[:, :]), waits=[bfin])
        Dop(lambda d: d.dma_start(out=dbg_d.ap()[:, 18:22], in_=amax_c[:, :]), waits=[bfin])
        Dop(lambda d: d.dma_start(out=dbg_d.ap()[:, 22:26], in_=stats_m[:, 0:4]), waits=[bfin])
        Dop(lambda d: d.dma_start(out=dbg_d.ap()[:, 26:30], in_=stats_r[:, 0:4]), waits=[bfin])
        S.op("sync", lambda d: d.nop(), waits=[("sd", S.cnt["sd"])])

        import os
        _lim = int(os.environ.get("KOPS", "999999999"))
        if _lim < len(S.ops):
            S.ops = S.ops[:_lim]
        with nc.Block() as block:
            def runner(eng_name):
                def run(e):
                    for eng, fn, ws, inc in S.ops:
                        if eng != eng_name:
                            continue
                        for snm, v in ws:
                            e.wait_ge(sems[snm], v)
                        instr = fn(e)
                        if inc:
                            snm, n = inc
                            tgt = instr if instr is not None else e.nop()
                            tgt.then_inc(sems[snm], n)
                return run

            block.tensor(runner("tensor"))
            block.vector(runner("vector"))
            block.scalar(runner("scalar"))
            block.sync(runner("sync"))
            block.gpsimd(runner("gpsimd"))

    return nc


_NC_CACHE = {}


def _consts():
    ident = np.eye(128, dtype=np.float32)
    msl = np.tril(np.ones((128, 128), np.float32), -1)
    msu = np.triu(np.ones((128, 128), np.float32), 1)
    mti = np.triu(np.ones((128, 128), np.float32), 0)
    masks = np.concatenate([np.tile(msl, (1, 4)), np.tile(msu, (1, 4)), np.tile(mti, (1, 4))], axis=1)
    lsel = np.zeros((16, 128), np.float32)
    for h in range(16):
        lsel[h, (h % 2) * 64:(h % 2) * 64 + 64] = 1.0
    rmask = np.zeros((16, 8), np.float32)
    for h in range(16):
        rmask[h, h // 2] = 1.0
    rep = np.zeros((16, 1024), np.float32)
    for h in range(16):
        rep[h, h * 64:(h + 1) * 64] = 1.0
    ones1 = np.ones((1, 128), np.float32)
    return dict(c_ident=ident, c_masks=masks, c_lsel=lsel, c_rmask=rmask, c_rep=rep, c_ones1=ones1)


def kernel(**inputs):
    x = np.ascontiguousarray(inputs["L0_attn_a_x"], dtype=np.float32)
    key = SUBLAYERS
    if key not in _NC_CACHE:
        _NC_CACHE[key] = build_nc(SUBLAYERS)
    nc = _NC_CACHE[key]
    base = dict(
        wq=np.ascontiguousarray(inputs["L0_attn_a_Wq"], np.float32),
        wk=np.ascontiguousarray(inputs["L0_attn_a_Wk"], np.float32),
        wv=np.ascontiguousarray(inputs["L0_attn_a_Wv"], np.float32),
        wo=np.ascontiguousarray(inputs["L0_attn_a_Wo"], np.float32),
        alpha=np.ascontiguousarray(inputs["L0_attn_a_alpha_raw"], np.float32).reshape(16, 1),
        beta=np.ascontiguousarray(inputs["L0_attn_a_beta_raw"], np.float32).reshape(16, 1),
        wqkv=np.ascontiguousarray(inputs["L0_attn_b_Wqkv"], np.float32),
        bqkv=np.ascontiguousarray(inputs["L0_attn_b_bqkv"], np.float32),
        wout=np.ascontiguousarray(inputs["L0_attn_b_Wout"], np.float32),
        bout=np.ascontiguousarray(inputs["L0_attn_b_bout"], np.float32),
        proj=np.ascontiguousarray(inputs["L0_attn_b_proj"], np.float32),
        wffn=np.ascontiguousarray(inputs["L0_ffn_weight"], np.float32),
        bffn=np.ascontiguousarray(inputs["L0_ffn_bias"], np.float32),
        **_consts(),
    )
    in_maps = []
    for core in range(NCORE):
        m = dict(base)
        m["x"] = np.ascontiguousarray(x[core * B_PER_CORE:(core + 1) * B_PER_CORE])
        in_maps.append(m)
    res = run_bass_kernel_spmd(nc, in_maps, core_ids=list(range(NCORE)))
    out = np.concatenate([r["y"] for r in res.results], axis=0)
    return out.astype(np.float32)


# version 03
# speedup vs baseline: 1.0027x; 1.0027x over previous
import contextlib
import numpy as np
import concourse.bass as bass
import concourse.mybir as mybir
import bass_rust as _br
from concourse.bass_utils import run_bass_kernel_spmd

F32 = mybir.dt.float32
BF16 = mybir.dt.bfloat16
AF = mybir.ActivationFunctionType
ALU = mybir.AluOpType
AX = mybir.AxisListType

SUBLAYERS = 4

B_PER_CORE = 2
T = 2048
C = 1024
H = 16
D = 64
CS = 128
NT = T // CS
GROUP = 4
NG = NT // GROUP
NCORE = 8
EPS = 1e-5
VC = float(2 ** 20 + 1)
VK = float(1.5 * 2 ** 14)


class Sched:
    def __init__(self):
        self.ops = []
        self.cnt = {}
        self.lastwait = {}

    DMA_SEMS = ("sd", "sg")

    def op(self, eng, fn, waits=(), inc=None, inc_n=1):
        ws = []
        for snm, v in tokset(waits):
            have = self.cnt.get(snm, 0)
            assert v <= have, f"wait on future: {eng} {snm} {v} > {have}"
            if snm in self.DMA_SEMS:
                v = have
            k = (eng, snm)
            if self.lastwait.get(k, 0) < v:
                self.lastwait[k] = v
                ws.append((snm, v))
        tok = None
        if inc:
            self.cnt[inc] = self.cnt.get(inc, 0) + inc_n
            tok = (inc, self.cnt[inc])
        self.ops.append((eng, fn, tuple(ws), (inc, inc_n) if inc else None))
        return tok

    def barrier(self):
        return tuple(sorted(self.cnt.items()))


def tokset(*toks):
    best = {}

    def add(t):
        if t is None:
            return
        if isinstance(t, tuple) and len(t) == 2 and isinstance(t[0], str):
            if best.get(t[0], -1) < t[1]:
                best[t[0]] = t[1]
            return
        for x in t:
            add(x)

    for t in toks:
        add(t)
    return tuple(sorted(best.items()))


later = tokset


def build_nc(sublayers):
    nc = bass.Bass("TRN2", target_bir_lowering=False)
    x_in = nc.dram_tensor("x", [B_PER_CORE, T, C], F32, kind="ExternalInput")
    wq_d = nc.dram_tensor("wq", [C, C], F32, kind="ExternalInput")
    wk_d = nc.dram_tensor("wk", [C, C], F32, kind="ExternalInput")
    wv_d = nc.dram_tensor("wv", [C, C], F32, kind="ExternalInput")
    wo_d = nc.dram_tensor("wo", [C, C], F32, kind="ExternalInput")
    alpha_d = nc.dram_tensor("alpha", [16, 1], F32, kind="ExternalInput")
    beta_d = nc.dram_tensor("beta", [16, 1], F32, kind="ExternalInput")
    wqkv_d = nc.dram_tensor("wqkv", [3 * C, C], F32, kind="ExternalInput")
    bqkv_d = nc.dram_tensor("bqkv", [3 * C], F32, kind="ExternalInput")
    wout_d = nc.dram_tensor("wout", [C, C], F32, kind="ExternalInput")
    bout_d = nc.dram_tensor("bout", [C], F32, kind="ExternalInput")
    proj_d = nc.dram_tensor("proj", [H, D, D], F32, kind="ExternalInput")
    wffn_d = nc.dram_tensor("wffn", [C, C], F32, kind="ExternalInput")
    bffn_d = nc.dram_tensor("bffn", [C], F32, kind="ExternalInput")
    c_ident = nc.dram_tensor("c_ident", [128, 128], F32, kind="ExternalInput")
    c_masks = nc.dram_tensor("c_masks", [128, 1536], F32, kind="ExternalInput")
    c_lsel = nc.dram_tensor("c_lsel", [16, 128], F32, kind="ExternalInput")
    c_rmask = nc.dram_tensor("c_rmask", [16, 8], F32, kind="ExternalInput")
    c_rep = nc.dram_tensor("c_rep", [16, 1024], F32, kind="ExternalInput")
    c_ones1 = nc.dram_tensor("c_ones1", [1, 128], F32, kind="ExternalInput")
    out_d = nc.dram_tensor("y", [B_PER_CORE, T, C], F32, kind="ExternalOutput")
    dbg_d = nc.dram_tensor("dbg", [128, 32], F32, kind="ExternalOutput")
    dbg2_d = nc.dram_tensor("dbg2", [128, 4096], F32, kind="ExternalOutput")
    park_d = nc.dram_tensor("park", [B_PER_CORE, T, C], F32)
    wqkvT_h = nc.dram_tensor("wqkvT_h", [C, 3 * C], BF16)
    woutT_h = nc.dram_tensor("woutT_h", [C, C], BF16)
    wq8T_h = nc.dram_tensor("wq8T_h", [C, C], BF16)
    cc_in = [nc.dram_tensor(f"cc_in{i}", [1, 8], F32) for i in range(2)]
    cc_out = [nc.dram_tensor(f"cc_out{i}", [1, 8], F32) for i in range(2)]

    with contextlib.ExitStack() as stack:
        def sb(name, shape, dt=BF16):
            return stack.enter_context(nc.sbuf_tensor(name, shape, dt))

        x_act = sb("x_act", [128, NT * C], F32)
        arena = sb("arena", [128, 45056], BF16)
        f32a = sb("f32a", [128, 5120], F32)
        masks = sb("masksb", [128, 1536], BF16)
        ident = sb("identb", [128, 128], BF16)
        S_f32 = sb("S_f32", [128, 512], F32)
        S_bf = sb("S_bf", [128, 512], BF16)
        kv_f32 = sb("kv_f32", [128, 1040], F32)
        kv_bf = sb("kv_bf", [128, 1040], BF16)
        projh = sb("projh", [128, 512], BF16)
        a2col = sb("a2col", [128, 8], F32)
        bb_bcast = sb("bb_bcast", [128, 1024], BF16)
        vbias_b = sb("vbias_b", [128, 1024], BF16)
        fbias_b = sb("fbias_b", [128, 1024], BF16)
        obias_b = sb("obias_b", [128, 1024], BF16)
        bqk_cols = sb("bqk_cols", [128, 16], F32)
        stats_m = sb("stats_m", [128, 32], F32)
        stats_r = sb("stats_r", [128, 32], F32)
        lncols = sb("lncols", [128, 16], F32)
        amax_c = sb("amax_c", [128, 4], F32)
        qcols = sb("qcols", [128, 2], F32)
        cell = sb("cell", [1, 16], F32)
        arow = sb("arow", [1, 128], F32)
        ones1p = sb("ones1p", [1, 128], F32)
        xln_bf = sb("xln_bf", [128, 1024], BF16)
        a16 = f32a[0:16, 2312:2313]
        b16 = f32a[0:16, 2313:2314]
        lsel = f32a[0:16, 1024:1152]
        rmask = f32a[0:16, 1152:1160]
        repm = f32a[0:16, 0:1024]
        ones1 = f32a[0:1, 1160:1288]
        rowt = f32a[0:1, 1288:2312]

        pA = stack.enter_context(nc.psum_tensor("pA", [128, 1024], F32))
        pB = stack.enter_context(nc.psum_tensor("pB", [128, 1024], F32))
        p4 = stack.enter_context(nc.psum_tensor("p4", [128, 512], F32))
        pT = stack.enter_context(nc.psum_tensor("pTt", [128, 1024], BF16))
        pj = stack.enter_context(nc.psum_tensor("pj", [128, 1024], F32))

        sems = {}
        for snm in ("sd", "sg", "st", "sv", "ss", "scc"):
            sems[snm] = stack.enter_context(nc.semaphore(snm))

        S = Sched()

        def Top(fn, waits=()):
            return S.op("tensor", fn, waits, "st")

        def Vop(fn, waits=()):
            return S.op("vector", fn, waits, "sv")

        def Sop(fn, waits=()):
            return S.op("scalar", fn, waits, "ss")

        def Dop(fn, waits=()):
            return S.op("sync", fn, waits, "sd", 16)

        def Gdma(fn, waits=()):
            return S.op("gpsimd", fn, waits, "sg", 16)

        def Gcop(fn, waits=()):
            return S.op("gpsimd", fn, waits, "sg", 1)

        t_ident = Gdma(lambda g: g.dma_start(out=ident[:, :], in_=c_ident.ap()))
        t_masks = Gdma(lambda g: g.dma_start(out=masks[:, :], in_=c_masks.ap()))
        t_lsel = Dop(lambda d: d.dma_start(out=lsel[:, :], in_=c_lsel.ap()))
        t_rmask = Dop(lambda d: d.dma_start(out=rmask[:, :], in_=c_rmask.ap()))
        t_rep = Dop(lambda d: d.dma_start(out=repm[:, :], in_=c_rep.ap()))
        t_ones = Dop(lambda d: d.dma_start(out=ones1[:, :], in_=c_ones1.ap()))
        t_onesp = Dop(lambda d: d.dma_start(out=ones1p[:, :], in_=c_ones1.ap()))
        t_al = Dop(lambda d: d.dma_start(out=a16[:, :], in_=alpha_d.ap()))
        t_be = Dop(lambda d: d.dma_start(out=b16[:, :], in_=beta_d.ap()))
        t_bqk = Dop(lambda d: d.dma_start(out=bqk_cols[:, :],
                                          in_=bqkv_d.ap()[0:2048].rearrange("(j p) -> p j", p=128),
                                          allow_slow_non_contiguous=True))
        t_projh = Gdma(lambda g: g.dma_start(
            out=projh[:, :].rearrange("p (j m) -> p j m", m=64),
            in_=proj_d.ap().rearrange("(j two) d m -> (two d) j m", two=2),
            allow_slow_non_contiguous=True))

        def sigmoid_half(dst16, src16, wait):
            zc = cell[0:16, 8:9] if False else None
            z = f32a[0:16, 2314:2315]
            p = f32a[0:16, 2315:2316]
            t0 = Vop(lambda v: v.tensor_scalar(z[:, :], src16, -1.0 / 1024.0, None, op0=ALU.mult), waits=[wait])
            tk = Vop(lambda v: v.tensor_scalar(p[:, :], z[:, :], 1.0 / 6.0, 1.0, op0=ALU.mult, op1=ALU.add),
                     waits=[t0])
            for kk in (5, 4, 3, 2, 1):
                tk = Vop(lambda v, kk=kk: v.scalar_tensor_tensor(p[:, :], z[:, :], 1.0 / kk, p[:, :],
                                                                 op0=ALU.mult, op1=ALU.mult), waits=[tk])
                tk = Vop(lambda v: v.tensor_scalar(p[:, :], p[:, :], 1.0, None, op0=ALU.add), waits=[tk])
            for _ in range(10):
                tk = Vop(lambda v: v.tensor_tensor(p[:, :], p[:, :], p[:, :], op=ALU.mult), waits=[tk])
            tk = Vop(lambda v: v.tensor_scalar(p[:, :], p[:, :], 1.0, None, op0=ALU.add), waits=[tk])
            tk = Vop(lambda v: v.reciprocal(p[:, :], p[:, :]), waits=[tk])
            return Vop(lambda v, d=dst16: v.tensor_scalar(d, p[:, :], 0.5, None, op0=ALU.mult), waits=[tk])

        t_a2 = sigmoid_half(a16[0:16, :], a16[0:16, :], t_al)
        t_b2 = sigmoid_half(b16[0:16, :], b16[0:16, :], t_be)
        t_ar = Vop(lambda v: v.tensor_scalar(rmask[0:16, :], rmask[0:16, :], a16[0:16, :], None, op0=ALU.mult),
                   waits=[t_a2, t_rmask])
        t_mm1 = Top(lambda t: t.matmul(pA[:, 0:8], lsel[0:16, :], rmask[0:16, :], start=True, stop=True),
                    waits=[t_ar, t_lsel])
        t_a2c = Vop(lambda v: v.tensor_copy(a2col[:, :], pA[:, 0:8]), waits=[t_mm1])
        t_br1 = Top(lambda t: t.matmul(pA[0:1, 512:1024], b16[0:16, :], repm[0:16, 0:512], start=True, stop=True),
                    waits=[t_b2, t_rep, t_a2c])
        t_br2 = Top(lambda t: t.matmul(pB[0:1, 0:512], b16[0:16, :], repm[0:16, 512:1024], start=True, stop=True))
        t_brow1 = Vop(lambda v: v.tensor_copy(rowt[0:1, 0:512], pA[0:1, 512:1024]), waits=[t_br1])
        t_brow2 = Vop(lambda v: v.tensor_copy(rowt[0:1, 512:1024], pB[0:1, 0:512]), waits=[t_br2])
        t_bb1 = Top(lambda t: t.matmul(pA[:, 0:512], ones1[0:1, :], rowt[0:1, 0:512], start=True, stop=True),
                    waits=[t_brow1, t_brow2, t_ones])
        t_bb2 = Top(lambda t: t.matmul(pA[:, 512:1024], ones1[0:1, :], rowt[0:1, 512:1024], start=True, stop=True))
        t_bb = Vop(lambda v: v.tensor_copy(bb_bcast[:, :], pA[:, :]), waits=[t_bb1, t_bb2])

        def build_bcast(dst, row_ap, after):
            tl = Dop(lambda d, r=row_ap: d.dma_start(out=rowt[0:1, :], in_=r), waits=[after])
            m1 = Top(lambda t: t.matmul(pA[:, 0:512], ones1[0:1, :], rowt[0:1, 0:512], start=True, stop=True),
                     waits=[tl, after])
            m2 = Top(lambda t: t.matmul(pA[:, 512:1024], ones1[0:1, :], rowt[0:1, 512:1024], start=True, stop=True))
            return Vop(lambda v, a=dst: v.tensor_copy(a, pA[:, :]), waits=[m1, m2])

        t_vb = build_bcast(vbias_b[:, :], bqkv_d.ap()[2048:3072].rearrange("(one c) -> one c", one=1), t_bb)
        t_fb = build_bcast(fbias_b[:, :], bffn_d.ap().rearrange("(one c) -> one c", one=1), t_vb)
        t_ob = build_bcast(obias_b[:, :], bout_d.ap().rearrange("(one c) -> one c", one=1), t_fb)

        wstage = arena[:, 0:8192]
        wf32 = f32a[:, 0:4096]
        u_ap = f32a[:, 4096:5120]
        t_am0 = Vop(lambda v: v.memset(amax_c[:, 1:2], 0.0))
        t_prev = t_ob
        for half in range(2):
            tl = Gdma(lambda g, hh=half: g.dma_start(
                out=wf32[:, :].rearrange("p (a c) -> p a c", c=1024),
                in_=wffn_d.ap().rearrange("(a p) c -> p a c", p=128)[:, hh * 4:(hh + 1) * 4, :]),
                waits=[t_prev])
            tr = Vop(lambda v: v.tensor_reduce(lncols[:, 0:1], wf32[:, :], AX.X, ALU.max,
                                               apply_absolute_value=True), waits=[tl, t_am0])
            t_prev = Vop(lambda v: v.tensor_tensor(amax_c[:, 1:2], amax_c[:, 1:2], lncols[:, 0:1], op=ALU.max),
                         waits=[tr])
        t_par0 = Gdma(lambda g: g.dma_start(out=arow[0:1, :], in_=amax_c[:, 1:2],
                                            allow_slow_non_contiguous=True), waits=[t_prev])
        t_par = Vop(lambda v: v.tensor_reduce(cell[0:1, 0:1], arow[0:1, :], AX.X, ALU.max), waits=[t_par0])
        t_sw1 = Vop(lambda v: v.tensor_scalar(cell[0:1, 1:2], cell[0:1, 0:1], 1e-12, None, op0=ALU.max),
                    waits=[t_par])
        t_sw2 = Vop(lambda v: v.reciprocal(cell[0:1, 2:3], cell[0:1, 1:2]), waits=[t_sw1])
        t_sw = Vop(lambda v: v.tensor_scalar(cell[0:1, 3:4], cell[0:1, 2:3], 448.0, None, op0=ALU.mult),
                   waits=[t_sw2])
        t_swb_m = Top(lambda t: t.matmul(pB[:, 512:513], ones1p[0:1, :], cell[0:1, 3:4], start=True, stop=True),
                      waits=[t_sw, t_brow2, t_onesp])
        t_swb = Vop(lambda v: v.tensor_copy(amax_c[:, 2:3], pB[:, 512:513]), waits=[t_swb_m])
        tq_done = t_swb
        for half in range(2):
            tl = Gdma(lambda g, hh=half: g.dma_start(
                out=wf32[:, :].rearrange("p (a c) -> p a c", c=1024),
                in_=wffn_d.ap().rearrange("(a p) c -> p a c", p=128)[:, hh * 4:(hh + 1) * 4, :]),
                waits=[tq_done])
            ts1 = Vop(lambda v: v.tensor_scalar(wf32[:, :], wf32[:, :], amax_c[:, 2:3], None, op0=ALU.mult),
                      waits=[tl])
            prev = ts1
            for strip in range(4):
                slc = slice(strip * 1024, (strip + 1) * 1024)
                t_u = Vop(lambda v, slc=slc: v.tensor_scalar(u_ap[:, 0:1024], wf32[:, slc], VK, VK,
                                                             op0=ALU.add, op1=ALU.subtract), waits=[prev])
                t_p = Vop(lambda v, slc=slc: v.tensor_scalar(wf32[:, slc], u_ap[:, 0:1024], VC, None,
                                                             op0=ALU.mult), waits=[t_u])
                t_d = Vop(lambda v, slc=slc: v.tensor_tensor(u_ap[:, 0:1024], u_ap[:, 0:1024], wf32[:, slc],
                                                             op=ALU.subtract), waits=[t_p])
                prev = Vop(lambda v, slc=slc, hh=half: v.tensor_tensor(
                    wstage[:, hh * 4096 + slc.start: hh * 4096 + slc.start + 1024],
                    u_ap[:, 0:1024], wf32[:, slc], op=ALU.add), waits=[t_d])
            tq_done = prev
        S.mark_wq = len(S.ops)
        pT_free = None
        tw_last = tq_done
        for a_i in range(8):
            trs = []
            for cb in range(8):
                trs.append(Top(lambda t, a_i=a_i, cb=cb: t.transpose(
                    pT[:, cb * 128: cb * 128 + 128],
                    wstage[:, a_i * 1024 + cb * 128: a_i * 1024 + cb * 128 + 128],
                    ident[:, :]), waits=[tq_done, t_ident, pT_free] if cb == 0 else ()))
            ev = Vop(lambda v: v.tensor_copy(xln_bf[:, 0:1024], pT[:, :]), waits=[trs[-1], tw_last])
            pT_free = ev
            tw_last = Dop(lambda d, a_i=a_i: d.dma_start(
                out=wq8T_h.ap().rearrange("(cb j) o -> j cb o", j=128)[:, :, a_i * 128:(a_i + 1) * 128],
                in_=xln_bf[:, 0:1024].rearrange("j (cb o) -> j cb o", o=128)), waits=[ev])
        t_wq8 = tw_last

        def build_wT(dst_h, src_d, rows, after):
            tok = after
            pTf = pT_free
            for a_i in range(rows // 128):
                tl = Gdma(lambda g, a_i=a_i, sdd=src_d: g.dma_start(
                    out=xln_bf[:, 0:1024], in_=sdd.ap()[a_i * 128:(a_i + 1) * 128, :]), waits=[tok])
                trs = []
                for cb in range(8):
                    trs.append(Top(lambda t, cb=cb: t.transpose(
                        pT[:, cb * 128: cb * 128 + 128],
                        xln_bf[:, cb * 128: cb * 128 + 128], ident[:, :]),
                        waits=[tl, pTf] if cb == 0 else ()))
                ev = Vop(lambda v: v.tensor_copy(wstage[:, 0:1024], pT[:, :]), waits=[trs[-1], tok])
                pTf = ev
                tok = Dop(lambda d, a_i=a_i, dd=dst_h: d.dma_start(
                    out=dd.ap().rearrange("(cb j) r -> j cb r", j=128)[:, :, a_i * 128:(a_i + 1) * 128],
                    in_=wstage[:, 0:1024].rearrange("j (cb o) -> j cb o", o=128)), waits=[ev])
            return tok, pTf

        t_wqkvT, pT_free = build_wT(wqkvT_h, wqkv_d, 3 * C, t_wq8)
        S.mark_prep = None
        t_woutT, pT_free = build_wT(woutT_h, wout_d, C, t_wqkvT)
        t_prep = t_woutT
        S.mark_prep = len(S.ops)

        x_tok = [None] * NT

        def load_seq(s, from_park, after):
            src = park_d if from_park else x_in
            toks = []
            for ti in range(NT):
                toks.append(Dop(lambda d, s=s, ti=ti, src=src: d.dma_start(
                    out=x_act[:, ti * C:(ti + 1) * C],
                    in_=src.ap()[s, ti * 128:(ti + 1) * 128, :]),
                    waits=[after, x_tok[ti]]))
                x_tok[ti] = toks[-1]
            return toks

        def park_seq(s):
            toks = []
            for ti in range(NT):
                toks.append(Dop(lambda d, s=s, ti=ti: d.dma_start(
                    out=park_d.ap()[s, ti * 128:(ti + 1) * 128, :],
                    in_=x_act[:, ti * C:(ti + 1) * C]), waits=[x_tok[ti]]))
            return toks

        def out_seq(s):
            toks = []
            for ti in range(NT):
                toks.append(Dop(lambda d, s=s, ti=ti: d.dma_start(
                    out=out_d.ap()[s, ti * 128:(ti + 1) * 128, :],
                    in_=x_act[:, ti * C:(ti + 1) * C]), waits=[x_tok[ti]]))
            return toks

        trash = f32a[:, 4096:5120]

        def newton_sqrt(dst, w_ap, t1_ap, h_ap, wait, iters=5):
            t0 = Vop(lambda v, h=h_ap, w=w_ap: v.tensor_scalar(h, w, 0.5, None, op0=ALU.mult), waits=[wait])
            tk = Vop(lambda v, d=dst, w=w_ap: v.tensor_scalar(d, w, 1.0, 0.5, op0=ALU.add, op1=ALU.mult),
                     waits=[t0])
            for _ in range(iters):
                tk = Vop(lambda v, t1=t1_ap, d=dst: v.reciprocal(t1, d), waits=[tk])
                tk = Vop(lambda v, t1=t1_ap, h=h_ap: v.tensor_tensor(t1, h, t1, op=ALU.mult), waits=[tk])
                tk = Vop(lambda v, d=dst, t1=t1_ap: v.scalar_tensor_tensor(d, d, 0.5, t1,
                                                                           op0=ALU.mult, op1=ALU.add),
                         waits=[tk])
            return tk

        def phase_stats(base, extra_wait=None):
            mcols = stats_m[:, base:base + NT]
            rcols = stats_r[:, base:base + NT]
            toks = [None]
            for ti in range(NT):
                xt = x_act[:, ti * C:(ti + 1) * C]
                w0 = [x_tok[ti], extra_wait, toks[-1]]
                toks.append(Vop(lambda v, xt=xt, ti=ti, m=mcols: v.tensor_reduce(
                    m[:, ti:ti + 1], xt, AX.X, ALU.add), waits=w0))
                toks.append(Vop(lambda v, xt=xt, ti=ti: v.scalar_tensor_tensor(
                    trash[:, 0:1024], xt, 1.0, xt, op0=ALU.mult, op1=ALU.mult,
                    accum_out=lncols[:, ti:ti + 1]), waits=[toks[-1]]))
            tm = Vop(lambda v, m=mcols: v.tensor_scalar(m, m, 1.0 / C, None, op0=ALU.mult), waits=[toks[-1]])
            t4 = Vop(lambda v, m=mcols: v.tensor_tensor(trash[:, 0:16], m, m, op=ALU.mult), waits=[tm])
            t5 = Vop(lambda v: v.scalar_tensor_tensor(trash[:, 16:32], lncols[:, 0:16], 1.0 / C,
                                                      trash[:, 0:16], op0=ALU.mult, op1=ALU.subtract),
                     waits=[t4])
            t6 = Vop(lambda v: v.tensor_scalar(trash[:, 16:32], trash[:, 16:32], EPS, None, op0=ALU.add),
                     waits=[t5])
            t7 = Vop(lambda v: v.reciprocal(trash[:, 16:32], trash[:, 16:32]), waits=[t6])
            tr = newton_sqrt(rcols, trash[:, 16:32], trash[:, 32:48], trash[:, 48:64], t7)
            return tokset(tm, tr)

        def attn_a_seq(first, after_tok, pT_free0):
            A = 0

            def take(n):
                nonlocal A
                ap = arena[:, A:A + n]
                A += n
                assert A <= 45056
                return ap

            wslot = [take(8192), take(8192)]
            xlnT = take(4096)
            qT = take(4096)
            kTn = take(4096)
            kTnA = take(4096)
            ktokn = take(4096)
            vb = take(4096)
            Yt = xlnT[:, 0:1024]
            YT = xlnT[:, 1024:2048]
            Xb = [wslot[0][:, i * 1024:(i + 1) * 1024] for i in range(4)]
            ub = [wslot[0][:, 4096:4608], wslot[0][:, 4608:5120]]
            PTb = wslot[0][:, 5120:6144]
            ktokr = f32a[:, 0:4096]

            tz1 = Vop(lambda v: v.memset(S_f32[:, :], 0.0), waits=[after_tok])
            tz2 = Sop(lambda s: s.copy(S_bf[:, :], S_f32[:, :]), waits=[tz1])
            sbf_tok = tz2

            stats_tok = phase_stats(0, after_tok)
            xlnbf_free = [None]
            wnames = [wq_d, wk_d, wv_d, wo_d]
            wfree = {0: after_tok, 1: after_tok}
            bank = {"pA": after_tok, "pB": after_tok, "p4": after_tok,
                    "pT": tokset(pT_free0, after_tok), "pj0": after_tok, "pj1": after_tok}
            ktokn_tok = [None] * GROUP
            tile_done = [None] * NT

            for g in range(NG):
                taus = list(range(g * GROUP, (g + 1) * GROUP))
                tq = Gdma(lambda gg, wd=wnames[0]: gg.dma_start(
                    out=wslot[0][:, :].rearrange("p (ci c) -> p ci c", c=1024),
                    in_=wd.ap().rearrange("(ci p) c -> p ci c", p=128)), waits=[wfree[0]])
                tk = Gdma(lambda gg, wd=wnames[1]: gg.dma_start(
                    out=wslot[1][:, :].rearrange("p (ci c) -> p ci c", c=1024),
                    in_=wd.ap().rearrange("(ci p) c -> p ci c", p=128)), waits=[wfree[1]])

                xlnT_toks = []
                for il, ti in enumerate(taus):
                    t_ln = Vop(lambda v, ti=ti: v.tensor_scalar(
                        xln_bf[:, :], x_act[:, ti * C:(ti + 1) * C],
                        stats_m[:, ti:ti + 1], stats_r[:, ti:ti + 1],
                        op0=ALU.subtract, op1=ALU.mult),
                        waits=[stats_tok, x_tok[ti], xlnbf_free[0]])
                    trs = []
                    for cb in range(8):
                        trs.append(Top(lambda t, cb=cb: t.transpose(
                            pT[:, cb * 128: cb * 128 + 128],
                            xln_bf[:, cb * 128: cb * 128 + 128], ident[:, :]),
                            waits=[t_ln, bank["pT"], t_ident] if cb == 0 else ()))
                    xlnbf_free[0] = trs[-1]
                    ev = Vop(lambda v, il=il: v.tensor_copy(
                        xlnT[:, :].rearrange("p (ci tt) -> p ci tt", tt=512)[:, :, il * 128:(il + 1) * 128],
                        pT[:, :].rearrange("p (ci tt) -> p ci tt", tt=128)), waits=[trs[-1]])
                    bank["pT"] = ev
                    xlnT_toks.append(ev)
                xlnT_all = later(*xlnT_toks)

                q_toks = []
                for di in range(8):
                    pjx = pj[:, 0:512] if di % 2 == 0 else pj[:, 512:1024]
                    bf = "pj0" if di % 2 == 0 else "pj1"
                    mms = []
                    for ci in range(8):
                        mms.append(Top(lambda t, di=di, ci=ci, pjx=pjx: t.matmul(
                            pjx, wslot[0][:, ci * 1024 + di * 128: ci * 1024 + di * 128 + 128],
                            xlnT[:, ci * 512:(ci + 1) * 512], start=(ci == 0), stop=(ci == 7)),
                            waits=[tq, xlnT_all, bank[bf]] if ci == 0 else ()))
                    ev = Sop(lambda s, di=di, pjx=pjx: s.copy(qT[:, di * 512:(di + 1) * 512], pjx),
                             waits=[mms[-1]])
                    bank[bf] = ev
                    q_toks.append(ev)
                q_done = later(*q_toks)
                q_mm_done = q_toks[-1]
                tv = Gdma(lambda gg, wd=wnames[2]: gg.dma_start(
                    out=wslot[0][:, :].rearrange("p (ci c) -> p ci c", c=1024),
                    in_=wd.ap().rearrange("(ci p) c -> p ci c", p=128)), waits=[q_mm_done])

                kgrp_toks = []
                for il in range(GROUP):
                    evs = []
                    for oh in range(2):
                        pjx = pj[:, 0:512] if oh == 0 else pj[:, 512:1024]
                        bf = "pj0" if oh == 0 else "pj1"
                        mms = []
                        for ci in range(8):
                            mms.append(Top(lambda t, il=il, oh=oh, ci=ci, pjx=pjx: t.matmul(
                                pjx, xlnT[:, ci * 512 + il * 128: ci * 512 + il * 128 + 128],
                                wslot[1][:, ci * 1024 + oh * 512: ci * 1024 + oh * 512 + 512],
                                start=(ci == 0), stop=(ci == 7)),
                                waits=[tk, xlnT_all, bank[bf]] if ci == 0 else ()))
                        ev = Vop(lambda v, il=il, oh=oh, pjx=pjx: v.tensor_copy(
                            ktokr[:, il * 1024 + oh * 512: il * 1024 + oh * 512 + 512], pjx), waits=[mms[-1]])
                        bank[bf] = ev
                        evs.append(ev)
                    nrm2 = lncols[:, 0:16]
                    rk = trash[:, 64:80]
                    t1 = Vop(lambda v, il=il: v.tensor_tensor(
                        trash[:, 0:1024], ktokr[:, il * 1024:(il + 1) * 1024],
                        ktokr[:, il * 1024:(il + 1) * 1024], op=ALU.mult),
                        waits=[evs[0], evs[1]])
                    t2 = Vop(lambda v: v.tensor_reduce(
                        nrm2, trash[:, 0:1024].rearrange("p (h dd) -> p h dd", dd=64), AX.X, ALU.add),
                        waits=[t1])
                    t3 = Vop(lambda v: v.tensor_scalar(nrm2, nrm2, 1e-24, None, op0=ALU.max),
                             waits=[t2])
                    t4 = Vop(lambda v: v.reciprocal(nrm2, nrm2), waits=[t3])
                    t5 = newton_sqrt(rk, nrm2, trash[:, 80:96], trash[:, 96:112], t4)
                    nts = []
                    for h in range(H):
                        nts.append(Vop(lambda v, il=il, h=h, rk=rk: v.tensor_scalar(
                            ktokn[:, il * 1024 + h * 64: il * 1024 + h * 64 + 64],
                            ktokr[:, il * 1024 + h * 64: il * 1024 + h * 64 + 64],
                            rk[:, h:h + 1], None, op0=ALU.mult), waits=[t5]))
                    kn = later(*nts)
                    ktokn_tok[il] = kn
                    trs = []
                    for j in range(8):
                        trs.append(Top(lambda t, il=il, j=j: t.transpose(
                            pT[:, j * 128: j * 128 + 128],
                            ktokn[:, il * 1024 + j * 128: il * 1024 + j * 128 + 128], ident[:, :]),
                            waits=[kn, bank["pT"]] if j == 0 else ()))
                    e1 = Vop(lambda v, il=il: v.tensor_copy(
                        kTn[:, :].rearrange("p (j tt) -> p j tt", tt=512)[:, :, il * 128: il * 128 + 128],
                        pT[:, :].rearrange("p (j tt) -> p j tt", tt=128)), waits=[trs[-1]])
                    e2s = []
                    for j in range(8):
                        e2s.append(Vop(lambda v, il=il, j=j: v.tensor_scalar(
                            kTnA[:, j * 512 + il * 128: j * 512 + il * 128 + 128],
                            pT[:, j * 128:(j + 1) * 128],
                            a2col[:, j:j + 1], None, op0=ALU.mult),
                            waits=[trs[-1], t_a2c] if j == 0 else ()))
                    bank["pT"] = tokset(e1, e2s[-1])
                    kgrp_toks.append(tokset(e1, e2s[-1]))
                k_done = later(*kgrp_toks)
                k_mm_done = kgrp_toks[-1]
                to = Gdma(lambda gg, wd=wnames[3]: gg.dma_start(
                    out=wslot[1][:, :].rearrange("p (ci c) -> p ci c", c=1024),
                    in_=wd.ap().rearrange("(ci p) c -> p ci c", p=128)), waits=[k_mm_done])

                v_toks = []
                for il in range(GROUP):
                    for oh in range(2):
                        pjx = pj[:, 0:512] if oh == 0 else pj[:, 512:1024]
                        bf = "pj0" if oh == 0 else "pj1"
                        mms = []
                        for ci in range(8):
                            mms.append(Top(lambda t, il=il, oh=oh, ci=ci, pjx=pjx: t.matmul(
                                pjx, xlnT[:, ci * 512 + il * 128: ci * 512 + il * 128 + 128],
                                wslot[0][:, ci * 1024 + oh * 512: ci * 1024 + oh * 512 + 512],
                                start=(ci == 0), stop=(ci == 7)),
                                waits=[tv, bank[bf]] if ci == 0 else ()))
                        ev = Vop(lambda v, il=il, oh=oh, pjx=pjx: v.tensor_tensor(
                            vb[:, il * 1024 + oh * 512: il * 1024 + oh * 512 + 512], pjx,
                            bb_bcast[:, oh * 512:(oh + 1) * 512], op=ALU.mult), waits=[mms[-1], t_bb])
                        bank[bf] = ev
                        v_toks.append(ev)
                v_done = later(*v_toks)

                for il, ti in enumerate(taus):
                    for b in range(2):
                        heads = list(range(b * 8, b * 8 + 8))
                        perm = [0, 2, 4, 6, 1, 3, 5, 7]
                        inv = [0, 4, 1, 5, 2, 6, 3, 7]
                        gmm = []
                        for sl in range(8):
                            i = perm[sl]
                            h = heads[i]
                            ph = (h % 2) * 64
                            jj = h // 2
                            gmm.append(Top(lambda t, sl=sl, ph=ph, jj=jj, il=il: t.matmul(
                                pA[:, sl * 128:(sl + 1) * 128],
                                kTnA[ph:ph + 64, jj * 512 + il * 128: jj * 512 + il * 128 + 128],
                                kTn[ph:ph + 64, jj * 512 + il * 128: jj * 512 + il * 128 + 128],
                                start=True, stop=True),
                                waits=[k_done, bank["pA"], t_masks] if sl == 0 else ()))
                        mN = []
                        for hl in range(2):
                            mN.append(Vop(lambda v, hl=hl: v.tensor_tensor(
                                Xb[0][:, hl * 512:(hl + 1) * 512], pA[:, hl * 512:(hl + 1) * 512],
                                masks[:, 0:512], op=ALU.mult), waits=[gmm[-1]]))
                            mN.append(Vop(lambda v, hl=hl: v.tensor_tensor(
                                Xb[1][:, hl * 512:(hl + 1) * 512], pA[:, hl * 512:(hl + 1) * 512],
                                masks[:, 512:1024], op=ALU.mult), waits=[gmm[-1]]))
                        bank["pA"] = later(*mN)
                        cur_tok = bank["pA"]
                        curX, curXT, nxtX, nxtXT = Xb[0], Xb[1], Xb[2], Xb[3]
                        ks = []
                        for sl in range(8):
                            i = perm[sl]
                            h = heads[i]
                            ph = (h % 2) * 64
                            jj = h // 2
                            dst = p4[:, sl * 64:(sl + 1) * 64] if sl < 4 else pj[:, (sl - 4) * 64:(sl - 3) * 64]
                            w0 = [sbf_tok, bank["p4"], bank["pj0"]] if sl == 0 else ()
                            ks.append(Top(lambda t, dst=dst, ph=ph, jj=jj, il=il: t.matmul(
                                dst,
                                kTnA[ph:ph + 64, jj * 512 + il * 128: jj * 512 + il * 128 + 128],
                                S_bf[ph:ph + 64, jj * 64:(jj + 1) * 64], start=True, stop=True),
                                waits=w0))
                        vview = vb[:, il * 1024 + b * 512: il * 1024 + b * 512 + 512].rearrange(
                            "p (four two d) -> p four two d", two=2, d=64)
                        tR1 = Vop(lambda v, vview=vview: v.tensor_tensor(
                            ub[0][:, 0:256].rearrange("p (four d) -> p four d", d=64),
                            vview[:, :, 0, :], p4[:, 0:256].rearrange("p (four d) -> p four d", d=64),
                            op=ALU.subtract), waits=[ks[-1], v_done])
                        tR2 = Vop(lambda v, vview=vview: v.tensor_tensor(
                            ub[0][:, 256:512].rearrange("p (four d) -> p four d", d=64),
                            vview[:, :, 1, :], pj[:, 0:256].rearrange("p (four d) -> p four d", d=64),
                            op=ALU.subtract), waits=[ks[-1], v_done])
                        tR = later(tR1, tR2)
                        bank["p4"] = tR
                        bank["pj0"] = tR
                        ucur, unxt = ub[0], ub[1]
                        u_tok = tR
                        for st_i in range(7):
                            ap_mms = []
                            for i in range(8):
                                ap_mms.append(Top(lambda t, i=i, cxt=curXT, uc=ucur: t.matmul(
                                    p4[:, i * 64:(i + 1) * 64],
                                    cxt[:, i * 128:(i + 1) * 128],
                                    uc[:, i * 64:(i + 1) * 64], start=True, stop=True),
                                    waits=[cur_tok, u_tok, bank["p4"]] if i == 0 else ()))
                            pw_ev = None
                            if st_i < 5:
                                pw = []
                                for i in range(8):
                                    pw.append(Top(lambda t, i=i, cx=curX, cxt=curXT: t.matmul(
                                        pB[:, i * 128:(i + 1) * 128],
                                        cxt[:, i * 128:(i + 1) * 128],
                                        cx[:, i * 128:(i + 1) * 128], start=True, stop=True),
                                        waits=[cur_tok, bank["pB"]] if i == 0 else ()))
                                for i in range(8):
                                    pw.append(Top(lambda t, i=i, cx=curX, cxt=curXT: t.matmul(
                                        pA[:, i * 128:(i + 1) * 128],
                                        cx[:, i * 128:(i + 1) * 128],
                                        cxt[:, i * 128:(i + 1) * 128], start=True, stop=True),
                                        waits=[bank["pA"]] if i == 0 else ()))
                                e1 = Vop(lambda v, nx=nxtX: v.tensor_copy(nx[:, :], pB[:, :]), waits=[pw[7]])
                                e2 = Sop(lambda s, nxt=nxtXT: s.copy(nxt[:, :], pA[:, :]), waits=[pw[-1]])
                                bank["pB"] = e1
                                bank["pA"] = e2
                                pw_ev = tokset(e1, e2)
                            elif st_i == 5:
                                pw = []
                                for i in range(8):
                                    pw.append(Top(lambda t, i=i, cx=curX, cxt=curXT: t.matmul(
                                        pB[:, i * 128:(i + 1) * 128],
                                        cx[:, i * 128:(i + 1) * 128],
                                        cxt[:, i * 128:(i + 1) * 128], start=True, stop=True),
                                        waits=[cur_tok, bank["pB"]] if i == 0 else ()))
                                e2 = Sop(lambda s, nxt=nxtXT: s.copy(nxt[:, :], pB[:, :]), waits=[pw[-1]])
                                bank["pB"] = e2
                                pw_ev = tokset(e2)
                            opx = ALU.subtract if st_i == 0 else ALU.add
                            tu = Vop(lambda v, opx=opx, un=unxt, uc=ucur: v.tensor_tensor(
                                un[:, :], uc[:, :], p4[:, :], op=opx), waits=[ap_mms[-1]])
                            bank["p4"] = tu
                            u_tok = tu
                            ucur, unxt = unxt, ucur
                            if st_i < 6 and pw_ev is not None:
                                curX, nxtX = nxtX, curX
                                curXT, nxtXT = nxtXT, curXT
                                cur_tok = pw_ev
                        W_sb = ucur
                        W_tok = u_tok
                        a2 = []
                        for sl in range(8):
                            i = perm[sl]
                            h = heads[i]
                            ph = (h % 2) * 64
                            jj = h // 2
                            a2.append(Top(lambda t, sl=sl, ph=ph, jj=jj, il=il: t.matmul(
                                pA[:, sl * 128:(sl + 1) * 128],
                                kTn[ph:ph + 64, jj * 512 + il * 128: jj * 512 + il * 128 + 128],
                                qT[ph:ph + 64, jj * 512 + il * 128: jj * 512 + il * 128 + 128],
                                start=True, stop=True),
                                waits=[q_done, bank["pA"]] if sl == 0 else ()))
                        tp1 = Vop(lambda v: v.tensor_tensor(PTb[:, 0:512], pA[:, 0:512],
                                                            masks[:, 512:1024], op=ALU.mult), waits=[a2[-1]])
                        tp2 = Vop(lambda v: v.tensor_tensor(PTb[:, 512:1024], pA[:, 512:1024],
                                                            masks[:, 512:1024], op=ALU.mult), waits=[a2[-1]])
                        bank["pA"] = later(tp1, tp2)
                        pt_tok = bank["pA"]
                        ym = []
                        for sl in range(8):
                            i = perm[sl]
                            h = heads[i]
                            ph = (h % 2) * 64
                            jj = h // 2
                            off = sl * 64 if sl < 4 else 512 + (sl - 4) * 64
                            ym.append(Top(lambda t, off=off, ph=ph, jj=jj, il=il: t.matmul(
                                pB[:, off:off + 64],
                                qT[ph:ph + 64, jj * 512 + il * 128: jj * 512 + il * 128 + 128],
                                S_bf[ph:ph + 64, jj * 64:(jj + 1) * 64], start=True, stop=True),
                                waits=[sbf_tok, bank["pB"]] if sl == 0 else ()))
                        ymi = []
                        for i in range(8):
                            ymi.append(Top(lambda t, i=i, W=W_sb: t.matmul(
                                p4[:, i * 64:(i + 1) * 64],
                                PTb[:, inv[i] * 128:(inv[i] + 1) * 128],
                                W[:, inv[i] * 64:(inv[i] + 1) * 64], start=True, stop=True),
                                waits=[pt_tok, W_tok, bank["p4"]] if i == 0 else ()))
                        yt_ev = Yt[:, b * 512: b * 512 + 512].rearrange("p (four two d) -> p four two d",
                                                                       two=2, d=64)
                        th1 = Sop(lambda s, yt_ev=yt_ev: s.copy(
                            yt_ev[:, :, 0, :], pB[:, 0:256].rearrange("p (four d) -> p four d", d=64)),
                            waits=[ym[-1]])
                        th2 = Sop(lambda s, yt_ev=yt_ev: s.copy(
                            yt_ev[:, :, 1, :], pB[:, 512:768].rearrange("p (four d) -> p four d", d=64)),
                            waits=[ym[-1]])
                        bank["pB"] = later(th1, th2)
                        ty = Vop(lambda v, b=b: v.tensor_tensor(
                            Yt[:, b * 512:(b + 1) * 512], Yt[:, b * 512:(b + 1) * 512],
                            p4[:, 0:512], op=ALU.add), waits=[th1, th2, ymi[-1]])
                        bank["p4"] = ty
                        su = []
                        for pass_par in (0, 1):
                            for i in range(pass_par, 8, 2):
                                h = heads[i]
                                jl = i // 2
                                if pass_par == 0:
                                    dst = p4[0:64, jl * 64:(jl + 1) * 64]
                                    tp = None
                                else:
                                    dst = pj[64:128, jl * 64:(jl + 1) * 64]
                                    tp = (0, 64)
                                w0 = [W_tok, ktokn_tok[il], bank["p4"], bank["pj0"]] if (pass_par == 0 and i == 0) else ()
                                su.append(Top(lambda t, dst=dst, h=h, i=i, tp=tp, il=il, W=W_sb: t.matmul(
                                    dst,
                                    ktokn[:, il * 1024 + h * 64: il * 1024 + h * 64 + 64],
                                    W[:, inv[i] * 64:(inv[i] + 1) * 64], start=True, stop=True,
                                    tile_position=tp), waits=w0))
                        tsu1 = Vop(lambda v, b=b: v.tensor_tensor(
                            S_f32[0:64, b * 256:(b + 1) * 256], S_f32[0:64, b * 256:(b + 1) * 256],
                            p4[0:64, 0:256], op=ALU.add), waits=[su[-1]])
                        tsu2 = Vop(lambda v, b=b: v.tensor_tensor(
                            S_f32[64:128, b * 256:(b + 1) * 256], S_f32[64:128, b * 256:(b + 1) * 256],
                            pj[64:128, 0:256], op=ALU.add), waits=[su[-1]])
                        bank["p4"] = tsu1
                        bank["pj0"] = tsu2
                        tsc = Sop(lambda s, b=b: s.copy(S_bf[:, b * 256:(b + 1) * 256],
                                                        S_f32[:, b * 256:(b + 1) * 256]),
                                  waits=[tsu1, tsu2, ym[-1]])
                        sbf_tok = tsc

                    ytr = []
                    for j in range(8):
                        ytr.append(Top(lambda t, j=j: t.transpose(
                            pT[:, j * 128:(j + 1) * 128],
                            Yt[:, j * 128:(j + 1) * 128], ident[:, :]),
                            waits=[bank["pB"], bank["pT"]] if j == 0 else ()))
                    evy = Vop(lambda v: v.tensor_copy(YT[:, :], pT[:, :]), waits=[ytr[-1]])
                    bank["pT"] = evy
                    rtoks = []
                    for oh in range(2):
                        pjx = pj[:, 0:512] if oh == 0 else pj[:, 512:1024]
                        bf = "pj0" if oh == 0 else "pj1"
                        mms = []
                        for ci in range(8):
                            mms.append(Top(lambda t, oh=oh, ci=ci, pjx=pjx: t.matmul(
                                pjx, YT[:, ci * 128:(ci + 1) * 128],
                                wslot[1][:, ci * 1024 + oh * 512: ci * 1024 + oh * 512 + 512],
                                start=(ci == 0), stop=(ci == 7)),
                                waits=[to, evy, bank[bf]] if ci == 0 else ()))
                        ev = Vop(lambda v, ti=ti, oh=oh, pjx=pjx: v.scalar_tensor_tensor(
                            x_act[:, ti * C + oh * 512: ti * C + oh * 512 + 512],
                            pjx, 1.0, x_act[:, ti * C + oh * 512: ti * C + oh * 512 + 512],
                            op0=ALU.mult, op1=ALU.add), waits=[mms[-1]])
                        bank[bf] = ev
                        rtoks.append(ev)
                        wo_mm_done = mms[-1]
                    x_tok[ti] = later(*rtoks)
                    tile_done[ti] = x_tok[ti]
                wfree[0] = tokset(sbf_tok, bank["p4"], bank["pB"], x_tok[taus[-1]])
                wfree[1] = wo_mm_done
            return tile_done, bank["pT"]

        def ffn_pass1(base, after):
            stats_tok = phase_stats(base, after)
            toks = [stats_tok]
            for ti in range(NT):
                t_mx = Vop(lambda v, ti=ti, base=base: v.tensor_scalar(
                    trash[:, 0:1024], x_act[:, ti * C:(ti + 1) * C],
                    stats_m[:, base + ti: base + ti + 1], -1e30,
                    op0=ALU.subtract, op1=ALU.max,
                    accum_out=amax_c[:, 2:3]), waits=[toks[-1], x_tok[ti]])
                t_mn = Vop(lambda v, ti=ti, base=base: v.tensor_scalar(
                    trash[:, 0:1024], x_act[:, ti * C:(ti + 1) * C],
                    stats_m[:, base + ti: base + ti + 1], 1e30,
                    op0=ALU.subtract, op1=ALU.min,
                    accum_out=amax_c[:, 3:4]), waits=[t_mx])
                t_ng = Vop(lambda v: v.tensor_scalar(amax_c[:, 3:4], amax_c[:, 3:4], -1.0, None,
                                                     op0=ALU.mult), waits=[t_mn])
                toks.append(Vop(lambda v, ti=ti: v.tensor_tensor(
                    lncols[:, ti:ti + 1], amax_c[:, 2:3], amax_c[:, 3:4], op=ALU.max), waits=[t_ng]))
            t1 = Vop(lambda v, base=base: v.tensor_tensor(
                lncols[:, 0:16], lncols[:, 0:16], stats_r[:, base:base + 16], op=ALU.mult),
                waits=[toks[-1]])
            t2 = Vop(lambda v: v.tensor_reduce(trash[:, 16:17], lncols[:, 0:16], AX.X, ALU.max),
                     waits=[t1])
            t3 = Vop(lambda v: v.tensor_tensor(amax_c[:, 0:1], amax_c[:, 0:1], trash[:, 16:17],
                                               op=ALU.max), waits=[t2])
            return t3

        def ffn_amax_finish(idx, after):
            t0 = Gdma(lambda g: g.dma_start(out=arow[0:1, :], in_=amax_c[:, 0:1],
                                            allow_slow_non_contiguous=True), waits=[after])
            t1 = Vop(lambda v: v.tensor_reduce(cell[0:1, 4:5], arow[0:1, :], AX.X, ALU.max),
                     waits=[t0])
            t1b = Vop(lambda v: v.memset(cell[0:1, 5:12], 0.0), waits=[t1])
            t2 = Dop(lambda d, idx=idx: d.dma_start(out=cc_in[idx].ap(), in_=cell[0:1, 4:12]),
                     waits=[t1, t1b])
            t3 = S.op("gpsimd", lambda g, idx=idx: g.collective_compute(
                "AllReduce", ALU.max, replica_groups=[list(range(NCORE))],
                ins=[cc_in[idx].ap().opt()], outs=[cc_out[idx].ap().opt()]),
                waits=[t2], inc="scc")
            t4 = Gdma(lambda g, idx=idx: g.dma_start(out=cell[0:1, 5:6], in_=cc_out[idx].ap()[0:1, 0:1]),
                      waits=[t3])
            t5 = Vop(lambda v: v.tensor_scalar(cell[0:1, 6:7], cell[0:1, 5:6], 1e-12, None, op0=ALU.max),
                     waits=[t4])
            t6 = Vop(lambda v: v.reciprocal(cell[0:1, 6:7], cell[0:1, 6:7]), waits=[t5])
            t7 = Vop(lambda v: v.tensor_scalar(cell[0:1, 6:7], cell[0:1, 6:7], 448.0, None, op0=ALU.mult),
                     waits=[t6])
            t8 = Vop(lambda v: v.tensor_tensor(cell[0:1, 7:8], cell[0:1, 6:7], cell[0:1, 3:4],
                                               op=ALU.mult), waits=[t7])
            t9 = Vop(lambda v: v.reciprocal(cell[0:1, 8:9], cell[0:1, 7:8]), waits=[t8])
            m1 = Top(lambda t: t.matmul(pB[:, 0:1], ones1p[0:1, :], cell[0:1, 6:7], start=True, stop=True),
                     waits=[t7, t_onesp, S.barrier()])
            m2 = Top(lambda t: t.matmul(pB[:, 512:513], ones1p[0:1, :], cell[0:1, 8:9],
                                        start=True, stop=True), waits=[t9])
            b1 = Vop(lambda v: v.tensor_copy(qcols[:, 0:1], pB[:, 0:1]), waits=[m1])
            b2 = Vop(lambda v: v.tensor_copy(qcols[:, 1:2], pB[:, 512:513]), waits=[m2])
            return tokset(b1, b2)

        def ffn_pass2(base, after, out_seq):
            A = 0

            def take(n):
                nonlocal A
                ap = arena[:, A:A + n]
                A += n
                assert A <= 45056
                return ap

            wq8 = take(8192)
            qxs = [take(1024), take(1024)]
            qxTs = [take(1024), take(1024)]
            trashes = [trash, f32a[:, 3072:4096]]
            twq = Dop(lambda d: d.dma_start(
                out=wq8[:, :].rearrange("p (ci o) -> p ci o", o=1024),
                in_=wq8T_h.ap().rearrange("(ci p) o -> p ci o", p=128)), waits=[after])
            trs = Vop(lambda v, base=base: v.tensor_scalar(
                lncols[:, 0:16], stats_r[:, base:base + 16], qcols[:, 0:1], None, op0=ALU.mult),
                waits=[after])
            bank = {"pT": S.barrier(), "pj0": S.barrier(), "pj1": S.barrier()}
            tq_free = [after, after]
            tqT_free = [after, after]
            t5_tok = {}
            tb_tok = {}
            out_toks = []

            def stageA(ti):
                par = ti % 2
                qx = qxs[par]
                trsh = trashes[par]
                tf = f32a[:, 0:1024]
                uf = f32a[:, 1024:2048]
                pf = f32a[:, 2048:3072]
                tbs = []
                for oh in range(2):
                    tbs.append(Vop(lambda v, ti=ti, oh=oh, trsh=trsh: v.tensor_tensor(
                        trsh[:, oh * 512:(oh + 1) * 512], fbias_b[:, oh * 512:(oh + 1) * 512],
                        x_act[:, ti * C + oh * 512: ti * C + oh * 512 + 512], op=ALU.add),
                        waits=[x_tok[ti]] if oh == 0 else ()))
                tb_tok[ti] = tbs
                t1 = Vop(lambda v, ti=ti, base=base: v.tensor_scalar(
                    tf[:, :], x_act[:, ti * C:(ti + 1) * C],
                    stats_m[:, base + ti: base + ti + 1], lncols[:, ti:ti + 1],
                    op0=ALU.subtract, op1=ALU.mult), waits=[trs, x_tok[ti]])
                t2 = Vop(lambda v: v.tensor_scalar(uf[:, :], tf[:, :], VK, VK,
                                                   op0=ALU.add, op1=ALU.subtract), waits=[t1])
                t3 = Vop(lambda v: v.tensor_scalar(pf[:, :], uf[:, :], VC, None, op0=ALU.mult),
                         waits=[t2])
                t4 = Vop(lambda v: v.tensor_tensor(uf[:, :], uf[:, :], pf[:, :], op=ALU.subtract),
                         waits=[t3])
                t5_tok[ti] = Vop(lambda v, qx=qx: v.tensor_tensor(
                    qx[:, :], uf[:, :], pf[:, :], op=ALU.add), waits=[t4, tq_free[par]])

            def stageB(ti):
                par = ti % 2
                qx = qxs[par]
                qxT = qxTs[par]
                trsh = trashes[par]
                qtrs = []
                for cb in range(8):
                    qtrs.append(Top(lambda t, cb=cb, qx=qx: t.transpose(
                        pT[:, cb * 128: cb * 128 + 128],
                        qx[:, cb * 128: cb * 128 + 128], ident[:, :]),
                        waits=[t5_tok[ti], bank["pT"], twq] if cb == 0 else ()))
                tq_free[par] = qtrs[-1]
                evq = Vop(lambda v, qxT=qxT: v.tensor_copy(qxT[:, :], pT[:, :]),
                          waits=[qtrs[-1], tqT_free[par]])
                bank["pT"] = evq
                rtoks = []
                for oh in range(2):
                    pjx = pj[:, 0:512] if oh == 0 else pj[:, 512:1024]
                    bf = "pj0" if oh == 0 else "pj1"
                    mms = []
                    for ci in range(8):
                        mms.append(Top(lambda t, oh=oh, ci=ci, pjx=pjx, qxT=qxT: t.matmul(
                            pjx, qxT[:, ci * 128:(ci + 1) * 128],
                            wq8[:, ci * 1024 + oh * 512: ci * 1024 + oh * 512 + 512],
                            start=(ci == 0), stop=(ci == 7)),
                            waits=[evq, bank[bf]] if ci == 0 else ()))
                    ev = Vop(lambda v, ti=ti, oh=oh, pjx=pjx, trsh=trsh: v.scalar_tensor_tensor(
                        x_act[:, ti * C + oh * 512: ti * C + oh * 512 + 512],
                        pjx, qcols[:, 1:2], trsh[:, oh * 512:(oh + 1) * 512],
                        op0=ALU.mult, op1=ALU.add), waits=[mms[-1], tb_tok[ti][oh]])
                    bank[bf] = ev
                    rtoks.append(ev)
                tqT_free[par] = rtoks[-1]
                x_tok[ti] = later(*rtoks)
                if out_seq is not None:
                    out_toks.append(Dop(lambda d, ti=ti, out_seq=out_seq: d.dma_start(
                        out=out_d.ap()[out_seq, ti * 128:(ti + 1) * 128, :],
                        in_=x_act[:, ti * C:(ti + 1) * C]), waits=[x_tok[ti]]))

            stageA(0)
            for ti in range(NT):
                if ti + 1 < NT:
                    stageA(ti + 1)
                stageB(ti)
            return out_toks

        def attn_b_seq(after_tok, pT_free0, dump_z=False):
            A = 0

            def take(n):
                nonlocal A
                ap = arena[:, A:A + n]
                A += n
                assert A <= 45056
                return ap

            wslot = [take(8192), take(8192)]
            xlnT = take(4096)
            qf = take(4096)
            kf = take(4096)
            vaug = take(4160)
            z = take(1024)
            zT = take(1024)
            qpT = take(1024)
            kpT = take(1024)
            kptok = take(1024)
            aw2m = take(1024)
            C_sb = f32a[:, 0:1024]

            stats_tok = phase_stats(0, after_tok)
            tz1 = Vop(lambda v: v.memset(kv_f32[:, :], 0.0), waits=[after_tok])
            tz2 = Sop(lambda s: s.copy(kv_bf[:, :], kv_f32[:, :]), waits=[tz1])
            kv_tok = tz2
            tz3s = []
            for blk in range(64):
                tz3s.append(Vop(lambda v, blk=blk: v.memset(
                    vaug[:, blk * 65 + 64: blk * 65 + 65], 1.0), waits=[after_tok] if blk == 0 else ()))
            tz3 = tz3s[-1]

            wsrc = [(wqkvT_h, 0), (wqkvT_h, 1024), (wqkvT_h, 2048), (woutT_h, 0)]
            wfree = {0: after_tok, 1: after_tok}
            bank = {"pA": after_tok, "pB": after_tok, "p4": after_tok,
                    "pT": tokset(pT_free0, after_tok), "pj0": after_tok, "pj1": after_tok}
            xlnbf_free = [None]
            tile_done = [None] * NT

            def load_w(slotidx, widx, extra=None):
                dsrc, coff = wsrc[widx]
                return Dop(lambda d, dsrc=dsrc, coff=coff, slotidx=slotidx: d.dma_start(
                    out=wslot[slotidx][:, :].rearrange("p (ci c) -> p ci c", c=1024),
                    in_=dsrc.ap().rearrange("(ci p) o -> p ci o", p=128)[:, :, coff:coff + 1024]),
                    waits=[wfree[slotidx], extra])

            for g in range(NG):
                taus = list(range(g * GROUP, (g + 1) * GROUP))
                tq = load_w(0, 0)
                tk = load_w(1, 1)
                xlnT_toks = []
                for il, ti in enumerate(taus):
                    t_ln = Vop(lambda v, ti=ti: v.tensor_scalar(
                        xln_bf[:, :], x_act[:, ti * C:(ti + 1) * C],
                        stats_m[:, ti:ti + 1], stats_r[:, ti:ti + 1],
                        op0=ALU.subtract, op1=ALU.mult),
                        waits=[stats_tok, x_tok[ti], xlnbf_free[0]])
                    ttrs = []
                    for cb in range(8):
                        ttrs.append(Top(lambda t, cb=cb: t.transpose(
                            pT[:, cb * 128: cb * 128 + 128],
                            xln_bf[:, cb * 128: cb * 128 + 128], ident[:, :]),
                            waits=[t_ln, bank["pT"]] if cb == 0 else ()))
                    xlnbf_free[0] = ttrs[-1]
                    ev = Vop(lambda v, il=il: v.tensor_copy(
                        xlnT[:, :].rearrange("p (ci tt) -> p ci tt", tt=512)[:, :, il * 128:(il + 1) * 128],
                        pT[:, :].rearrange("p (ci tt) -> p ci tt", tt=128)), waits=[ttrs[-1]])
                    bank["pT"] = ev
                    xlnT_toks.append(ev)
                xlnT_all = later(*xlnT_toks)

                def fproj(dstbuf, slotidx, wtok, bcol_base, extra=None):
                    toks = []
                    for di in range(8):
                        pjx = pj[:, 0:512] if di % 2 == 0 else pj[:, 512:1024]
                        bf = "pj0" if di % 2 == 0 else "pj1"
                        mms = []
                        for ci in range(8):
                            mms.append(Top(lambda t, di=di, ci=ci, pjx=pjx, slotidx=slotidx: t.matmul(
                                pjx, wslot[slotidx][:, ci * 1024 + di * 128: ci * 1024 + di * 128 + 128],
                                xlnT[:, ci * 512:(ci + 1) * 512], start=(ci == 0), stop=(ci == 7)),
                                waits=[wtok, xlnT_all, bank[bf], extra] if ci == 0 else ()))
                        ev = Vop(lambda v, di=di, pjx=pjx, dstbuf=dstbuf, bcol_base=bcol_base: v.tensor_scalar(
                            dstbuf[:, di * 512:(di + 1) * 512], pjx,
                            bqk_cols[:, bcol_base + di: bcol_base + di + 1], None, op0=ALU.add),
                            waits=[mms[-1], t_bqk])
                        bank[bf] = ev
                        toks.append(ev)
                    return toks

                q_toks = fproj(qf, 0, tq, 0)
                tv = load_w(0, 2, q_toks[-1])
                k_toks = fproj(kf, 1, tk, 8)
                to = load_w(1, 3, k_toks[-1])
                q_done = later(*q_toks)
                k_done = later(*k_toks)

                v_toks = []
                for il in range(GROUP):
                    for oh in range(2):
                        pjx = pj[:, 0:512] if oh == 0 else pj[:, 512:1024]
                        bf = "pj0" if oh == 0 else "pj1"
                        mms = []
                        for ci in range(8):
                            mms.append(Top(lambda t, il=il, oh=oh, ci=ci, pjx=pjx: t.matmul(
                                pjx, xlnT[:, ci * 512 + il * 128: ci * 512 + il * 128 + 128],
                                wslot[0][:, ci * 1024 + oh * 512: ci * 1024 + oh * 512 + 512],
                                start=(ci == 0), stop=(ci == 7)),
                                waits=[tv, bank[bf]] if ci == 0 else ()))
                        ev = Vop(lambda v, il=il, oh=oh, pjx=pjx: v.tensor_tensor(
                            vaug[:, il * 1040: il * 1040 + 1040].rearrange(
                                "p (h e) -> p h e", e=65)[:, oh * 8:(oh + 1) * 8, 0:64],
                            pjx.rearrange("p (h dd) -> p h dd", dd=64),
                            vbias_b[:, oh * 512:(oh + 1) * 512].rearrange("p (h dd) -> p h dd", dd=64),
                            op=ALU.add), waits=[mms[-1], t_vb, tz3])
                        bank[bf] = ev
                        v_toks.append(ev)
                v_done = later(*v_toks)
                wfree[0] = v_toks[-1]

                for il, ti in enumerate(taus):
                    for b in range(2):
                        heads = list(range(b * 8, b * 8 + 8))
                        perm = [0, 2, 4, 6, 1, 3, 5, 7]
                        qmm, kmm = [], []
                        for sl in range(8):
                            i = perm[sl]
                            h = heads[i]
                            ph = (h % 2) * 64
                            jj = h // 2
                            qmm.append(Top(lambda t, sl=sl, ph=ph, jj=jj, il=il: t.matmul(
                                pA[0:64, sl * 128:(sl + 1) * 128],
                                projh[ph:ph + 64, jj * 64:(jj + 1) * 64],
                                qf[ph:ph + 64, jj * 512 + il * 128: jj * 512 + il * 128 + 128],
                                start=True, stop=True),
                                waits=[q_done, t_projh, bank["pA"]] if sl == 0 else ()))
                        for sl in range(8):
                            i = perm[sl]
                            h = heads[i]
                            ph = (h % 2) * 64
                            jj = h // 2
                            kmm.append(Top(lambda t, sl=sl, ph=ph, jj=jj, il=il: t.matmul(
                                pB[0:64, sl * 128:(sl + 1) * 128],
                                projh[ph:ph + 64, jj * 64:(jj + 1) * 64],
                                kf[ph:ph + 64, jj * 512 + il * 128: jj * 512 + il * 128 + 128],
                                start=True, stop=True),
                                waits=[k_done, bank["pB"]] if sl == 0 else ()))
                        eq = Vop(lambda v: v.tensor_scalar(qpT[0:64, 0:1024], pA[0:64, 0:1024], 0.0, None,
                                                           op0=ALU.max), waits=[qmm[-1]])
                        ek = Vop(lambda v: v.tensor_scalar(kpT[0:64, 0:1024], pB[0:64, 0:1024], 0.0, None,
                                                           op0=ALU.max), waits=[kmm[-1]])
                        bank["pA"] = eq
                        bank["pB"] = ek
                        ktr = []
                        for sl in range(8):
                            ktr.append(Top(lambda t, sl=sl: t.transpose(
                                pT[:, sl * 64: sl * 64 + 64],
                                kpT[0:64, sl * 128:(sl + 1) * 128], ident[0:64, 0:64]),
                                waits=[ek, bank["pT"]] if sl == 0 else ()))
                        ekt = Vop(lambda v: v.tensor_copy(kptok[:, 0:512], pT[:, 0:512]),
                                  waits=[ktr[-1]])
                        bank["pT"] = ekt
                        amm = []
                        for sl in range(8):
                            amm.append(Top(lambda t, sl=sl: t.matmul(
                                pA[:, sl * 128:(sl + 1) * 128],
                                kpT[0:64, sl * 128:(sl + 1) * 128],
                                qpT[0:64, sl * 128:(sl + 1) * 128],
                                start=True, stop=True),
                                waits=[eq, ek, bank["pA"]] if sl == 0 else ()))
                        am1 = Vop(lambda v: v.tensor_tensor(aw2m[:, 0:512], pA[:, 0:512],
                                                            masks[:, 1024:1536], op=ALU.mult),
                                  waits=[amm[-1]])
                        am2 = Vop(lambda v: v.tensor_tensor(aw2m[:, 512:1024], pA[:, 512:1024],
                                                            masks[:, 1024:1536], op=ALU.mult),
                                  waits=[amm[-1]])
                        bank["pA"] = later(am1, am2)
                        hmm_ = []
                        for sl in range(8):
                            kvp = b * 8 + sl
                            off = sl * 66 if sl < 4 else 512 + (sl - 4) * 66
                            hmm_.append(Top(lambda t, sl=sl, kvp=kvp, off=off: t.matmul(
                                pB[:, off:off + 65],
                                qpT[0:64, sl * 128:(sl + 1) * 128],
                                kv_bf[0:64, kvp * 65:(kvp + 1) * 65],
                                start=True, stop=True),
                                waits=[kv_tok, eq, bank["pB"]] if sl == 0 else ()))
                        imm = []
                        for sl in range(8):
                            h = heads[perm[sl]]
                            dst = (p4[:, sl * 66: sl * 66 + 65] if sl < 4
                                   else pj[:, (sl - 4) * 66:(sl - 4) * 66 + 65])
                            w0 = [later(am1, am2), v_done, bank["p4"], bank["pj0"]] if sl == 0 else ()
                            imm.append(Top(lambda t, sl=sl, h=h, dst=dst, il=il: t.matmul(
                                dst, aw2m[:, sl * 128:(sl + 1) * 128],
                                vaug[:, il * 1040 + h * 65: il * 1040 + (h + 1) * 65],
                                start=True, stop=True), waits=w0))
                        cview = C_sb[:, 0:520].rearrange("p (s e) -> p s e", e=65)
                        c1 = Vop(lambda v, cview=cview: v.tensor_copy(
                            cview[:, 0:4, :], pB[:, 0:264].rearrange("p (s e) -> p s e", e=66)[:, :, 0:65]),
                            waits=[hmm_[-1]])
                        c2 = Vop(lambda v, cview=cview: v.tensor_copy(
                            cview[:, 4:8, :],
                            pB[:, 512:776].rearrange("p (s e) -> p s e", e=66)[:, :, 0:65]),
                            waits=[hmm_[-1]])
                        bank["pB"] = later(c1, c2)
                        c3 = Vop(lambda v, cview=cview: v.tensor_tensor(
                            cview[:, 0:4, :], cview[:, 0:4, :],
                            p4[:, 0:264].rearrange("p (s e) -> p s e", e=66)[:, :, 0:65],
                            op=ALU.add), waits=[c1, imm[3]])
                        c4 = Vop(lambda v, cview=cview: v.tensor_tensor(
                            cview[:, 4:8, :], cview[:, 4:8, :],
                            pj[:, 0:264].rearrange("p (s e) -> p s e", e=66)[:, :, 0:65],
                            op=ALU.add), waits=[c2, imm[-1]])
                        bank["p4"] = c3
                        bank["pj0"] = c4
                        den = f32a[:, 1040:1048]
                        rec = f32a[:, 1040:1048]
                        d0 = Vop(lambda v: v.memset(f32a[:, 1040:1056], 1.0), waits=[c3, c4])
                        d1s = [d0]
                        for sl in range(8):
                            d1s.append(Vop(lambda v, sl=sl, den=den: v.tensor_scalar(
                                den[:, sl:sl + 1], C_sb[:, sl * 65 + 64: sl * 65 + 65],
                                1e-6, None, op0=ALU.add), waits=[d0] if sl == 0 else ()))
                        d2 = Vop(lambda v: v.reciprocal(f32a[:, 1040 + 16:1040 + 32] if False else f32a[:, 1040:1056][:, 0:16], f32a[:, 1040:1056]), waits=[d1s[-1]])
                        zts = []
                        for sl in range(8):
                            h = heads[perm[sl]]
                            zts.append(Vop(lambda v, sl=sl, h=h, rec=rec: v.tensor_scalar(
                                z[:, h * 64:(h + 1) * 64], C_sb[:, sl * 65: sl * 65 + 64],
                                rec[:, sl:sl + 1], None, op0=ALU.mult), waits=[d2]))
                        z_tok = later(*zts)
                        nmm = []
                        for sl in range(8):
                            h = heads[perm[sl]]
                            dst = (p4[0:64, sl * 66: sl * 66 + 65] if sl < 4
                                   else pj[0:64, (sl - 4) * 66:(sl - 4) * 66 + 65])
                            w0 = [ekt, v_done, bank["p4"], bank["pj0"]] if sl == 0 else ()
                            nmm.append(Top(lambda t, sl=sl, h=h, dst=dst, il=il: t.matmul(
                                dst, kptok[:, sl * 64:(sl + 1) * 64],
                                vaug[:, il * 1040 + h * 65: il * 1040 + (h + 1) * 65],
                                start=True, stop=True), waits=w0))
                        base_c = b * 520
                        kv1 = Vop(lambda v, base_c=base_c: v.tensor_tensor(
                            kv_f32[0:64, base_c: base_c + 260].rearrange("p (s e) -> p s e", e=65),
                            kv_f32[0:64, base_c: base_c + 260].rearrange("p (s e) -> p s e", e=65),
                            p4[0:64, 0:264].rearrange("p (s e) -> p s e", e=66)[:, :, 0:65],
                            op=ALU.add), waits=[nmm[-1]])
                        kv2 = Vop(lambda v, base_c=base_c: v.tensor_tensor(
                            kv_f32[0:64, base_c + 260: base_c + 520].rearrange("p (s e) -> p s e", e=65),
                            kv_f32[0:64, base_c + 260: base_c + 520].rearrange("p (s e) -> p s e", e=65),
                            pj[0:64, 0:264].rearrange("p (s e) -> p s e", e=66)[:, :, 0:65],
                            op=ALU.add), waits=[nmm[-1]])
                        bank["p4"] = kv1
                        bank["pj0"] = kv2
                        kv_tok = Sop(lambda s, b=b: s.copy(kv_bf[0:64, b * 520:(b + 1) * 520],
                                                           kv_f32[0:64, b * 520:(b + 1) * 520]),
                                     waits=[kv1, kv2, hmm_[-1]])
                    if dump_z and ti == 1:
                        Gdma(lambda d: d.dma_start(out=dbg2_d.ap()[:, 0:1024], in_=z[:, :]),
                             waits=[z_tok])
                    ztr = []
                    for cb in range(8):
                        ztr.append(Top(lambda t, cb=cb: t.transpose(
                            pT[:, cb * 128: cb * 128 + 128],
                            z[:, cb * 128: cb * 128 + 128], ident[:, :]),
                            waits=[z_tok, bank["pT"]] if cb == 0 else ()))
                    evz = Vop(lambda v: v.tensor_copy(zT[:, :], pT[:, :]), waits=[ztr[-1]])
                    bank["pT"] = evz
                    rtoks = []
                    for oh in range(2):
                        pjx = pj[:, 0:512] if oh == 0 else pj[:, 512:1024]
                        bf = "pj0" if oh == 0 else "pj1"
                        mms = []
                        for ci in range(8):
                            mms.append(Top(lambda t, oh=oh, ci=ci, pjx=pjx: t.matmul(
                                pjx, zT[:, ci * 128:(ci + 1) * 128],
                                wslot[1][:, ci * 1024 + oh * 512: ci * 1024 + oh * 512 + 512],
                                start=(ci == 0), stop=(ci == 7)),
                                waits=[to, evz, bank[bf]] if ci == 0 else ()))
                        tb = Vop(lambda v, ti=ti, oh=oh: v.tensor_tensor(
                            trash[:, oh * 512:(oh + 1) * 512], obias_b[:, oh * 512:(oh + 1) * 512],
                            x_act[:, ti * C + oh * 512: ti * C + oh * 512 + 512], op=ALU.add),
                            waits=[mms[0], t_ob])
                        ev = Vop(lambda v, ti=ti, oh=oh, pjx=pjx: v.scalar_tensor_tensor(
                            x_act[:, ti * C + oh * 512: ti * C + oh * 512 + 512],
                            pjx, 1.0, trash[:, oh * 512:(oh + 1) * 512],
                            op0=ALU.mult, op1=ALU.add), waits=[mms[-1], tb])
                        bank[bf] = ev
                        rtoks.append(ev)
                    x_tok[ti] = later(*rtoks)
                    tile_done[ti] = x_tok[ti]
                wfree[1] = mms[-1]
            return tile_done, bank["pT"]

        load_seq(0, False, t_prep)
        done0, pT_free = attn_a_seq(True, S.barrier(), pT_free)
        park0 = park_seq(0)
        load_seq(1, False, S.barrier())
        done1, pT_free = attn_a_seq(False, S.barrier(), pT_free)

        if sublayers == 1:
            o1 = out_seq(1)
            o0 = []
            for ti in range(NT):
                o0.append(Dop(lambda d, ti=ti: d.dma_start(
                    out=out_d.ap()[0, ti * 128:(ti + 1) * 128, :],
                    in_=park_d.ap()[0, ti * 128:(ti + 1) * 128, :]), waits=[park0[ti]]))

        if sublayers >= 2:
            za = Vop(lambda v: v.memset(amax_c[:, 0:1], 0.0), waits=[S.barrier()])
            ffn_pass1(16, tokset(za, S.barrier()))
            park_seq(1)
            load_seq(0, True, S.barrier())
            ffn_pass1(0, S.barrier())
            ffn_amax_finish(0, S.barrier())
            out0 = 0 if sublayers == 2 else None
            ffn_pass2(0, S.barrier(), out0)
            park_seq(0)
            load_seq(1, True, S.barrier())
            ffn_pass2(16, S.barrier(), 1 if sublayers == 2 else None)

        if sublayers >= 3:
            done1b, pT_free = attn_b_seq(S.barrier(), pT_free)
            park1b = park_seq(1)
            load_seq(0, True, S.barrier())
            done0b, pT_free = attn_b_seq(S.barrier(), pT_free, dump_z=(sublayers == 3))
            if sublayers == 3:
                out_seq(0)
                for ti in range(NT):
                    Dop(lambda d, ti=ti: d.dma_start(
                        out=out_d.ap()[1, ti * 128:(ti + 1) * 128, :],
                        in_=park_d.ap()[1, ti * 128:(ti + 1) * 128, :]), waits=[park1b[ti]])

        if sublayers >= 4:
            za = Vop(lambda v: v.memset(amax_c[:, 0:1], 0.0), waits=[S.barrier()])
            ffn_pass1(0, tokset(za, S.barrier()))
            park_seq(0)
            load_seq(1, True, S.barrier())
            ffn_pass1(16, S.barrier())
            ffn_amax_finish(1, S.barrier())
            ffn_pass2(16, S.barrier(), 1)
            load_seq(0, True, S.barrier())
            ffn_pass2(0, S.barrier(), 0)

        bfin = S.barrier()
        if sublayers == 3:
            Gdma(lambda d: d.dma_start(out=dbg2_d.ap()[:, 1040:1056], in_=f32a[:, 1040:1056]), waits=[bfin])
            Gdma(lambda d: d.dma_start(out=dbg2_d.ap()[:, 1056:2080], in_=f32a[:, 0:1024]), waits=[bfin])
            Gdma(lambda d: d.dma_start(out=dbg2_d.ap()[:, 2080:3104], in_=arena[:, 34880:35904]), waits=[bfin])
            Gdma(lambda d: d.dma_start(out=dbg2_d.ap()[:, 3104:4096], in_=arena[:, 32832:33824]), waits=[bfin])
            Gdma(lambda d: d.dma_start(out=dbg2_d.ap()[:, 2080:3104], in_=arena[:, 33856:34880]), waits=[bfin])
        if sublayers == 2:
            Gdma(lambda d: d.dma_start(out=dbg2_d.ap()[:, 0:1024], in_=arena[:, 8192:9216]), waits=[bfin])
            Gdma(lambda d: d.dma_start(out=dbg2_d.ap()[:, 1024:2048], in_=arena[:, 9216:10240]), waits=[bfin])
            Gdma(lambda d: d.dma_start(out=dbg2_d.ap()[:, 2048:3072], in_=arena[:, 0:1024]), waits=[bfin])
            Dop(lambda d: d.dma_start(out=dbg2_d.ap()[:, 3072:4096], in_=f32a[:, 0:1024]), waits=[bfin])
        Dop(lambda d: d.dma_start(out=dbg_d.ap()[0:1, 0:16], in_=cell[0:1, :]), waits=[bfin])
        Dop(lambda d: d.dma_start(out=dbg_d.ap()[:, 16:18], in_=qcols[:, :]), waits=[bfin])
        Dop(lambda d: d.dma_start(out=dbg_d.ap()[:, 18:22], in_=amax_c[:, :]), waits=[bfin])
        Dop(lambda d: d.dma_start(out=dbg_d.ap()[:, 22:26], in_=stats_m[:, 0:4]), waits=[bfin])
        Dop(lambda d: d.dma_start(out=dbg_d.ap()[:, 26:30], in_=stats_r[:, 0:4]), waits=[bfin])
        S.op("sync", lambda d: d.nop(), waits=[("sd", S.cnt["sd"])])

        import os
        _lim = int(os.environ.get("KOPS", "999999999"))
        if _lim < len(S.ops):
            S.ops = S.ops[:_lim]
        with nc.Block() as block:
            def runner(eng_name):
                def run(e):
                    for eng, fn, ws, inc in S.ops:
                        if eng != eng_name:
                            continue
                        for snm, v in ws:
                            e.wait_ge(sems[snm], v)
                        instr = fn(e)
                        if inc:
                            snm, n = inc
                            tgt = instr if instr is not None else e.nop()
                            tgt.then_inc(sems[snm], n)
                return run

            block.tensor(runner("tensor"))
            block.vector(runner("vector"))
            block.scalar(runner("scalar"))
            block.sync(runner("sync"))
            block.gpsimd(runner("gpsimd"))

    return nc


_NC_CACHE = {}


def _consts():
    ident = np.eye(128, dtype=np.float32)
    msl = np.tril(np.ones((128, 128), np.float32), -1)
    msu = np.triu(np.ones((128, 128), np.float32), 1)
    mti = np.triu(np.ones((128, 128), np.float32), 0)
    masks = np.concatenate([np.tile(msl, (1, 4)), np.tile(msu, (1, 4)), np.tile(mti, (1, 4))], axis=1)
    lsel = np.zeros((16, 128), np.float32)
    for h in range(16):
        lsel[h, (h % 2) * 64:(h % 2) * 64 + 64] = 1.0
    rmask = np.zeros((16, 8), np.float32)
    for h in range(16):
        rmask[h, h // 2] = 1.0
    rep = np.zeros((16, 1024), np.float32)
    for h in range(16):
        rep[h, h * 64:(h + 1) * 64] = 1.0
    ones1 = np.ones((1, 128), np.float32)
    return dict(c_ident=ident, c_masks=masks, c_lsel=lsel, c_rmask=rmask, c_rep=rep, c_ones1=ones1)


def kernel(**inputs):
    x = np.ascontiguousarray(inputs["L0_attn_a_x"], dtype=np.float32)
    key = SUBLAYERS
    if key not in _NC_CACHE:
        _NC_CACHE[key] = build_nc(SUBLAYERS)
    nc = _NC_CACHE[key]
    base = dict(
        wq=np.ascontiguousarray(inputs["L0_attn_a_Wq"], np.float32),
        wk=np.ascontiguousarray(inputs["L0_attn_a_Wk"], np.float32),
        wv=np.ascontiguousarray(inputs["L0_attn_a_Wv"], np.float32),
        wo=np.ascontiguousarray(inputs["L0_attn_a_Wo"], np.float32),
        alpha=np.ascontiguousarray(inputs["L0_attn_a_alpha_raw"], np.float32).reshape(16, 1),
        beta=np.ascontiguousarray(inputs["L0_attn_a_beta_raw"], np.float32).reshape(16, 1),
        wqkv=np.ascontiguousarray(inputs["L0_attn_b_Wqkv"], np.float32),
        bqkv=np.ascontiguousarray(inputs["L0_attn_b_bqkv"], np.float32),
        wout=np.ascontiguousarray(inputs["L0_attn_b_Wout"], np.float32),
        bout=np.ascontiguousarray(inputs["L0_attn_b_bout"], np.float32),
        proj=np.ascontiguousarray(inputs["L0_attn_b_proj"], np.float32),
        wffn=np.ascontiguousarray(inputs["L0_ffn_weight"], np.float32),
        bffn=np.ascontiguousarray(inputs["L0_ffn_bias"], np.float32),
        **_consts(),
    )
    in_maps = []
    for core in range(NCORE):
        m = dict(base)
        m["x"] = np.ascontiguousarray(x[core * B_PER_CORE:(core + 1) * B_PER_CORE])
        in_maps.append(m)
    res = run_bass_kernel_spmd(nc, in_maps, core_ids=list(range(NCORE)))
    out = np.concatenate([r["y"] for r in res.results], axis=0)
    return out.astype(np.float32)
